# version 01
# baseline (speedup 1.0000x reference)
import sys
sys.path.insert(0, '/opt/trn_rl_repo')
from contextlib import ExitStack
import numpy as np
import ml_dtypes
import concourse.bass as bass
import concourse.mybir as mybir
from concourse.bass_utils import run_bass_kernel_spmd

F32 = mybir.dt.float32
BF16 = mybir.dt.bfloat16
AF = mybir.ActivationFunctionType
ALU = mybir.AluOpType
AX = mybir.AxisListType

T, HALO, CS, NCH = 1024, 3, 256, 4
DM, GATE, HS, NB, NCC, DT = 4096, 8192, 8192, 1024, 1024, 128
XBC = HS + NB + NCC
PROJ = GATE + XBC + DT
H, P, N, G = 128, 64, 128, 8
NCT_XBC, NCT_GATE = XBC // 128, GATE // 128
NCT = PROJ // 128
TH = T + HALO
NEG = -1e30
EPS = 1e-5

ENGS = ("sync", "tensor", "vector", "scalar", "gpsimd")


class Sem:
    def __init__(self, es, nc, name):
        self.h = es.enter_context(nc.semaphore(name))
        self.n = 0


class Prog:
    def __init__(self, nc, es):
        self.nc, self.es = nc, es
        self.q = {e: [] for e in ENGS}

    def sem(self, name):
        return Sem(self.es, self.nc, name)

    def wait(self, eng, sem, val=None):
        v = sem.n if val is None else val
        self.q[eng].append(lambda e, s=sem.h, v=v: e.wait_ge(s, v))

    def run(self, eng, fn, inc=None, k=1):
        if inc is not None:
            inc.n += k
            self.q[eng].append(lambda e, f=fn, s=inc.h, kk=k: f(e).then_inc(s, kk))
        else:
            self.q[eng].append(fn)

    def dma(self, eng, out, in_, inc):
        self.run(eng, lambda e, o=out, i=in_: e.dma_start(out=o, in_=i), inc, 16)

    def finalize(self):
        with self.nc.Block() as block:
            @block.sync
            def _(e):
                for f in self.q["sync"]: f(e)
            @block.tensor
            def _(e):
                for f in self.q["tensor"]: f(e)
            @block.vector
            def _(e):
                for f in self.q["vector"]: f(e)
            @block.scalar
            def _(e):
                for f in self.q["scalar"]: f(e)
            @block.gpsimd
            def _(e):
                for f in self.q["gpsimd"]: f(e)


def stage_inputs(inputs):
    x = np.ascontiguousarray(inputs["L0_attn_input_states"][0])
    W = inputs["L0_attn_in_proj_w"]
    Wre = np.concatenate([W[GATE:GATE + XBC], W[GATE + XBC:], W[:GATE]], axis=0)
    wip = np.ascontiguousarray(
        Wre.reshape(NCT, 128, 32, 128).transpose(0, 3, 2, 1))
    convw = np.ascontiguousarray(
        inputs["L0_attn_conv_w"][:, 0, :].reshape(80, 128, 4).transpose(1, 0, 2)
        .reshape(128, 320))
    convb = np.ascontiguousarray(
        inputs["L0_attn_conv_b"].reshape(80, 128).T)
    ident = np.eye(128, dtype=np.float32)
    maskd = np.where(np.arange(128)[None, :] >= np.arange(128)[:, None],
                     np.float32(0), np.float32(NEG))
    maskd4 = np.ascontiguousarray(np.tile(maskd, (1, 4)))
    shared = dict(
        wip=wip,
        convw=convw, convb=convb,
        dtb=np.ascontiguousarray(inputs["L0_attn_dt_bias"].reshape(128, 1)),
        dtbrow=np.ascontiguousarray(inputs["L0_attn_dt_bias"].reshape(1, 128)),
        alogcol=np.ascontiguousarray(inputs["L0_attn_A_log"].reshape(128, 1)),
        drow=np.ascontiguousarray(inputs["L0_attn_D"].reshape(1, 128)),
        identf=ident,
        identb=ident.astype(ml_dtypes.bfloat16),
        maskd4=maskd4,
        onesrow=np.ones((1, 128), np.float32),
    )
    per_core = []
    for k in range(8):
        sl = x[1024 * k:1024 * (k + 1)]
        xh = np.zeros((TH, DM), np.float32)
        lo = 1024 * k - HALO
        s0 = max(lo, 0)
        xh[s0 - lo:] = x[s0:1024 * (k + 1)]
        xfm = np.ascontiguousarray(xh.reshape(TH, 32, 128).transpose(2, 1, 0))
        per_core.append(dict(xfm=xfm))
    return shared, per_core


def stage_l2(inputs):
    outw = inputs["L0_attn_out_w"]
    wop = np.ascontiguousarray(
        outw.reshape(8, 512, 64, 128).transpose(0, 2, 3, 1))
    worg = inputs["L0_ffn_W_org"]
    wbase = np.ascontiguousarray(
        worg.reshape(8, 512, 32, 128).transpose(0, 2, 3, 1))
    down = inputs["L0_ffn_lora_down"]
    dcat = np.concatenate([down[e] for e in range(8)], axis=1)
    wdown = np.ascontiguousarray(
        dcat.reshape(32, 128, 4, 128).transpose(0, 1, 2, 3))
    wq = inputs["L0_ffn_Wq"]
    wqT = np.ascontiguousarray(wq.T.reshape(32, 128, 128))
    normw = np.ascontiguousarray(
        inputs["L0_attn_norm_w"].reshape(64, 128).T)
    return dict(wop=wop, wbase=wbase, wdown=wdown, wqT=wqT, normw=normw)


def stage_l3(inputs):
    up = inputs["L0_ffn_lora_up"]
    ucat = np.concatenate([up[e] for e in range(8)], axis=0)
    wup = np.ascontiguousarray(
        ucat.reshape(4, 128, 8, 512))
    keysrow = np.ascontiguousarray(
        inputs["L0_ffn_lora_keys"].reshape(1, 1024))
    return dict(wup=wup, keysrow=keysrow)


def build_l1(debug=False):
    nc = bass.Bass("TRN2", target_bir_lowering=False, debug=False)
    es = ExitStack()
    p = Prog(nc, es)

    I = {}
    for nm, sh, dt in [
        ("xfm", [128, 32, TH], F32), ("wip", [NCT, 128, 32, 128], F32),
        ("convw", [128, 320], F32), ("convb", [128, 80], F32),
        ("dtb", [128, 1], F32), ("dtbrow", [1, 128], F32),
        ("alogcol", [128, 1], F32), ("drow", [1, 128], F32),
        ("identf", [128, 128], F32), ("identb", [128, 128], BF16),
        ("maskd4", [128, 512], F32), ("onesrow", [1, 128], F32),
    ]:
        I[nm] = nc.dram_tensor(nm, sh, dt, kind="ExternalInput").ap()
    O = {}
    outs = [("yp", [HS, T], BF16), ("sg", [GATE, T], BF16),
            ("sst", [NCH, 128, 8192], F32), ("score", [128, 8192], F32),
            ("cum", [128, T], F32), ("clast", [NCH, 128], F32),
            ("cfm", [8, 128, T], BF16)]
    if debug:
        outs += [("dbg_xbcs", [80, 128, T], BF16), ("dbg_dtsp", [128, T], F32),
                 ("dbg_biastok", [128, T], F32), ("dbg_msc", [128, 128, 384], BF16),
                 ("dbg_sexp", [NCH, 128, 256], F32)]
    for nm, sh, dt in outs:
        O[nm] = nc.dram_tensor(nm, sh, dt, kind="ExternalOutput").ap()
    xbcst = nc.dram_tensor("xbcst", [80, 128, T], BF16, kind="Internal").ap()
    cumrows = nc.dram_tensor("cumrows", [NCH, 2, 128 * 128], F32, kind="Internal").ap()

    sb = lambda nm, sh, dt: es.enter_context(nc.sbuf_tensor(nm, sh, dt))
    S = {nm: p.sem(nm) for nm in
         ("cld xld wld mm evA cv sl stX stO so ta ev2 ax rl dq sc ms ym ye "
          "sm se pw").split()}
    s = lambda nm: S[nm]

    c_identf = sb("c_identf", [128, 128], F32)
    c_identb = sb("c_identb", [128, 128], BF16)
    c_maskd4 = sb("c_maskd4", [128, 512], F32)
    c_convw = sb("c_convw", [128, 320], F32)
    c_convb = sb("c_convb", [128, 80], F32)
    c_dtb = sb("c_dtb", [128, 1], F32)
    c_alog = sb("c_alog", [128, 1], F32)
    c_ones1 = sb("c_ones1", [1, 128], F32)
    c_onesT = sb("c_onesT", [128, 256], F32)
    r_dtbrow = sb("r_dtbrow", [1, 128], F32)
    r_drow = sb("r_drow", [1, 128], F32)
    dtbB = sb("dtbB", [128, 128], F32)
    Db = sb("Db", [128, 128], F32)
    cum_fm = sb("cum_fm", [128, T], F32)
    bias_tok = sb("bias_tok", [128, T], F32)
    xbc_ch = sb("xbc_ch", [128, 80 * 256], BF16)

    for nm, dst in [("identf", c_identf), ("identb", c_identb),
                    ("maskd4", c_maskd4), ("convw", c_convw),
                    ("convb", c_convb), ("dtb", c_dtb), ("alogcol", c_alog),
                    ("dtbrow", r_dtbrow), ("drow", r_drow)]:
        p.dma("sync", dst[:, :], I[nm][:, :], s("cld"))
    v_cld = s("cld").n
    p.run("vector", lambda e: e.memset(c_ones1[:, :], 1.0))
    p.run("vector", lambda e: e.memset(c_onesT[:, :], 1.0), s("ev2"))
    v_ones = s("ev2").n

    es1 = ExitStack()
    sb1 = lambda nm, sh, dt: es1.enter_context(nc.sbuf_tensor(nm, sh, dt))
    x_fm = sb1("x_fm", [128, 32 * TH], BF16)
    Wb = [sb1(f"Wb{i}", [128, 4096], BF16) for i in range(2)]
    raw = [sb1(f"raw{i}", [128, TH], F32) for i in range(2)]
    caccA = sb1("caccA", [128, T], F32)
    caccB = sb1("caccB", [128, T], F32)
    outt = [sb1(f"outt{i}", [128, T], BF16) for i in range(2)]
    dt_fm = sb1("dt_fm", [128, T], F32)
    t1f = sb1("t1f", [128, T], F32)
    t2f = sb1("t2f", [128, T], F32)
    cum_tok = sb1("cum_tok", [128, T], F32)

    es1p = ExitStack()
    ps1 = lambda nm, sh: es1p.enter_context(nc.psum_tensor(nm, sh, F32))
    PA0 = [ps1(f"PA0{i}", [128, 512]) for i in range(2)]
    PA1 = [ps1(f"PA1{i}", [128, 512]) for i in range(2)]
    PAt = [ps1(f"PAt{i}", [128, 512]) for i in range(2)]
    PTP = [ps1(f"PTP{i}", [128, 128]) for i in range(2)]

    p.dma("gpsimd", x_fm[:, :], I["xfm"].rearrange("p a t -> p (a t)"), s("xld"))
    p.wait("tensor", s("cld"), v_cld)
    p.wait("tensor", s("ev2"), v_ones)
    p.run("tensor", lambda e: e.matmul(PTP[0][:, :], c_ones1[:, :],
                                       r_dtbrow[:, :], start=True, stop=True))
    p.run("tensor", lambda e: e.matmul(PTP[1][:, :], c_ones1[:, :],
                                       r_drow[:, :], start=True, stop=True), s("ta"))
    p.wait("vector", s("ta"))
    p.run("vector", lambda e: e.tensor_copy(dtbB[:, :], PTP[0][:, :]))
    p.run("vector", lambda e: e.tensor_copy(Db[:, :], PTP[1][:, :]), s("ev2"))
    v_bc0 = s("ev2").n
    wb_free = [[], []]
    pa_free = [[], []]
    raw_free = [[], []]
    cacc_free = [[], []]
    outt_free = [[], []]
    ptp_free = [[("ev2", v_bc0)], [("ev2", v_bc0)]]

    def wmulti(eng, lst):
        for (nm, v) in lst:
            p.wait(eng, s(nm), v)

    def ctile(ct, kind):
        par = ct % 2
        wmulti("gpsimd", wb_free[par])
        p.dma("gpsimd", Wb[par][:, :],
              I["wip"][ct].rearrange("p a c -> p (a c)"), s("wld"))
        p.wait("tensor", s("wld"), 16 * s("wld").n // 16)
        if ct == 0:
            p.wait("tensor", s("xld"), 16)
        wmulti("tensor", pa_free[par])
        slices = ([(0, 512, PA0[par]), (512, 1024, PA1[par]), (1024, TH, PAt[par])]
                  if kind == "xbc" else [(3, 515, PA0[par]), (515, TH, PA1[par])])
        for si, (c0, c1, ps_) in enumerate(slices):
            w = c1 - c0
            last_slice = (si == len(slices) - 1)
            for dti in range(32):
                lhs = Wb[par][:, dti * 128:(dti + 1) * 128]
                rhs = x_fm[:, dti * TH + c0: dti * TH + c1]
                p.run("tensor",
                      lambda e, o=ps_[:, 0:w], l=lhs, r=rhs, st=(dti == 0),
                      sp=(dti == 31): e.matmul(o, l, r, start=st, stop=sp),
                      s("mm") if (last_slice and dti == 31) else None, 1)
        v_mm = s("mm").n
        wb_free[par] = [("mm", v_mm)]
        if kind == "xbc":
            wmulti("vector", raw_free[par])
            p.wait("vector", s("mm"), v_mm)
            p.run("vector", lambda e, pr=par: e.tensor_copy(raw[pr][:, 0:512],
                                                            PA0[pr][:, :]))
            p.run("vector", lambda e, pr=par: e.tensor_copy(raw[pr][:, 512:1024],
                                                            PA1[pr][:, :]))
            p.run("vector", lambda e, pr=par: e.tensor_copy(raw[pr][:, 1024:TH],
                                                            PAt[pr][:, 0:3]),
                  s("evA"))
            v_ev = s("evA").n
            pa_free[par] = [("evA", v_ev)]
            wmulti("vector", cacc_free[0])
            if ct == 0:
                p.wait("vector", s("cld"), v_cld)
            wc = lambda k: c_convw[:, ct * 4 + k: ct * 4 + k + 1]
            p.run("vector", lambda e, pr=par, w0=wc(0): e.tensor_scalar(
                caccA[:, :], raw[pr][:, 0:T], w0, None, ALU.mult))
            p.run("vector", lambda e, pr=par, w1=wc(1): e.scalar_tensor_tensor(
                caccB[:, :], raw[pr][:, 1:1 + T], w1, caccA[:, :],
                ALU.mult, ALU.add))
            p.run("vector", lambda e, pr=par, w2=wc(2): e.scalar_tensor_tensor(
                caccA[:, :], raw[pr][:, 2:2 + T], w2, caccB[:, :],
                ALU.mult, ALU.add))
            p.run("vector", lambda e, pr=par, w3=wc(3): e.scalar_tensor_tensor(
                caccB[:, :], raw[pr][:, 3:3 + T], w3, caccA[:, :],
                ALU.mult, ALU.add), s("cv"))
            v_cv = s("cv").n
            raw_free[par] = []
            wmulti("scalar", outt_free[par])
            p.wait("scalar", s("cv"), v_cv)
            p.run("scalar", lambda e, pr=par, b=c_convb[:, ct:ct + 1]:
                  e.activation(outt[pr][:, :], caccB[:, :], AF.Silu, bias=b),
                  s("sl"))
            v_sl = s("sl").n
            cacc_free[0] = [("sl", v_sl)]
            p.wait("sync", s("sl"), v_sl)
            p.dma("sync", xbcst[ct], outt[par][:, :], s("stX"))
            outt_free[par] = [("stX", s("stX").n)]
            if 72 <= ct < 80:
                p.dma("sync", O["cfm"][ct - 72], outt[par][:, :], s("stO"))
                outt_free[par].append(("stO", s("stO").n))
            if debug:
                p.dma("sync", O["dbg_xbcs"][ct], outt[par][:, :], s("stO"))
                outt_free[par].append(("stO", s("stO").n))
        elif kind == "gate":
            gi = ct - 81
            wmulti("scalar", outt_free[par])
            p.wait("scalar", s("mm"), v_mm)
            p.run("scalar", lambda e, pr=par: e.activation(
                outt[pr][:, 0:512], PA0[pr][:, :], AF.Silu))
            p.run("scalar", lambda e, pr=par: e.activation(
                outt[pr][:, 512:1024], PA1[pr][:, :], AF.Silu), s("evA"))
            v_ev = s("evA").n
            pa_free[par] = [("evA", v_ev)]
            p.wait("sync", s("evA"), v_ev)
            p.dma("sync", O["sg"][gi * 128:(gi + 1) * 128, :], outt[par][:, :],
                  s("stO"))
            outt_free[par] = [("stO", s("stO").n)]
        else:
            p.wait("vector", s("mm"), v_mm)
            p.run("vector", lambda e, pr=par: e.tensor_copy(dt_fm[:, 0:512],
                                                            PA0[pr][:, :]))
            p.run("vector", lambda e, pr=par: e.tensor_copy(dt_fm[:, 512:1024],
                                                            PA1[pr][:, :]),
                  s("evA"))
            pa_free[par] = [("evA", s("evA").n)]

    for ct in range(80):
        ctile(ct, "xbc")
    ctile(80, "dt")

    v_dtfm = s("evA").n
    p.wait("scalar", s("evA"), v_dtfm)
    p.run("scalar", lambda e: e.activation(t1f[:, :], dt_fm[:, :], AF.Relu,
                                           bias=c_dtb[:, :]))
    p.run("scalar", lambda e: e.activation(t2f[:, :], dt_fm[:, :], AF.Abs,
                                           bias=c_dtb[:, :]))
    p.run("scalar", lambda e: e.activation(t2f[:, :], t2f[:, :], AF.Exp,
                                           scale=-1.0))
    p.run("scalar", lambda e: e.activation(t2f[:, :], t2f[:, :], AF.Ln, bias=1.0))
    p.run("scalar", lambda e: e.activation(c_alog[:, :], c_alog[:, :], AF.Exp),
          s("ax"))
    p.wait("vector", s("ax"))
    p.run("vector", lambda e: e.tensor_add(dt_fm[:, :], t1f[:, :], t2f[:, :]),
          s("ev2"))
    if debug:
        p.wait("sync", s("ev2"), s("ev2").n)
        p.dma("sync", O["dbg_dtsp"][:, :], dt_fm[:, :], s("stO"))
    p.run("vector", lambda e: e.tensor_scalar(t1f[:, :], dt_fm[:, :],
                                              c_alog[:, :], -1.0,
                                              ALU.mult, ALU.mult))
    for c in range(NCH):
        p.run("vector", lambda e, c0=c * CS: e.tensor_tensor_scan(
            cum_fm[:, c0:c0 + CS], c_onesT[:, :], t1f[:, c0:c0 + CS], 0.0,
            ALU.mult, ALU.add), s("ev2"))
    v_cum = s("ev2").n
    p.wait("sync", s("ev2"), v_cum)
    p.dma("sync", O["cum"][:, :], cum_fm[:, :], s("stO"))
    for c in range(NCH):
        p.dma("sync", cumrows[c].rearrange("b (h i) -> h b i", i=128),
              cum_fm[:, c * CS:(c + 1) * CS].rearrange("h (b i) -> h b i", b=2),
              s("rl"))
    v_cumrows = s("rl").n
    p.wait("tensor", s("ev2"), v_cum)
    for tt in range(8):
        par = tt % 2
        wmulti("tensor", ptp_free[par])
        p.run("tensor", lambda e, pr=par, t0=tt * 128: e.transpose(
            PTP[pr][:, :], cum_fm[:, t0:t0 + 128], c_identf[:, :]), s("ta"))
        p.wait("vector", s("ta"), s("ta").n)
        p.run("vector", lambda e, pr=par, t0=tt * 128: e.tensor_copy(
            cum_tok[:, t0:t0 + 128], PTP[pr][:, :]), s("ev2"))
        ptp_free[par] = [("ev2", s("ev2").n)]
    for tt in range(8):
        par = tt % 2
        wmulti("tensor", pa_free[par])
        p.wait("tensor", s("ev2"), v_bc0)
        for dti in range(32):
            p.run("tensor", lambda e, pr=par, d=dti, t0=3 + tt * 128:
                  e.matmul(PAt[pr][:, 0:128],
                           x_fm[:, d * TH + t0: d * TH + t0 + 128],
                           Wb[0][:, d * 128:(d + 1) * 128],
                           start=(d == 0), stop=(d == 31)),
                  s("ta") if dti == 31 else None, 1)
        v_ta = s("ta").n
        p.wait("vector", s("ta"), v_ta)
        p.run("vector", lambda e, pr=par, t0=tt * 128: e.tensor_add(
            t2f[:, t0:t0 + 128], PAt[pr][:, 0:128], dtbB[:, :]), s("ev2"))
        pa_free[par] = [("ev2", s("ev2").n)]
        p.wait("scalar", s("ev2"), s("ev2").n)
        p.run("scalar", lambda e, t0=tt * 128: e.activation(
            t1f[:, t0:t0 + 128], t2f[:, t0:t0 + 128], AF.Abs))
        p.run("scalar", lambda e, t0=tt * 128: e.activation(
            t1f[:, t0:t0 + 128], t1f[:, t0:t0 + 128], AF.Exp, scale=-1.0))
        p.run("scalar", lambda e, t0=tt * 128: e.activation(
            t1f[:, t0:t0 + 128], t1f[:, t0:t0 + 128], AF.Ln, bias=1.0), s("ax"))
        p.wait("vector", s("ax"), s("ax").n)
        p.run("vector", lambda e, t0=tt * 128: e.tensor_scalar_max(
            bias_tok[:, t0:t0 + 128], t2f[:, t0:t0 + 128], 0.0))
        p.run("vector", lambda e, t0=tt * 128: e.tensor_add(
            t2f[:, t0:t0 + 128], bias_tok[:, t0:t0 + 128], t1f[:, t0:t0 + 128]),
            s("ev2"))
    wb_free[0] = [("ta", s("ta").n)]
    p.wait("scalar", s("ev2"), s("ev2").n)
    p.run("scalar", lambda e: e.activation(t1f[:, :], t2f[:, :], AF.Ln), s("ax"))
    p.wait("vector", s("ax"), s("ax").n)
    p.run("vector", lambda e: e.tensor_sub(bias_tok[:, :], t1f[:, :],
                                           cum_tok[:, :]), s("ev2"))
    v_biastok = s("ev2").n
    if debug:
        p.wait("sync", s("ev2"), v_biastok)
        p.dma("sync", O["dbg_biastok"][:, :], bias_tok[:, :], s("stO"))

    for ct in range(81, 145):
        ctile(ct, "gate")
    v_gate_mm = s("mm").n
    v_gate_stO = s("stO").n
    v_gate_ta = s("ta").n
    es1p.close()
    es1.close()
    p.finalize_l1_tail = None
    return _l1_tail(nc, es, p, I, O, xbcst, cumrows, sb, S, debug,
                    dict(c_identf=c_identf, c_identb=c_identb,
                         c_maskd4=c_maskd4, c_ones1=c_ones1, Db=Db,
                         cum_fm=cum_fm, bias_tok=bias_tok, xbc_ch=xbc_ch),
                    v_gate_mm, v_gate_stO, v_biastok, v_cum, v_cumrows)


def _l1_tail(nc, es, p, I, O, xbcst, cumrows, sb, S, debug, R,
             v_gate_mm, v_gate_stO, v_biastok, v_cum, v_cumrows):
    s = lambda nm: S[nm]
    c_identf, c_identb, c_maskd4 = R["c_identf"], R["c_identb"], R["c_maskd4"]
    c_ones1, Db = R["c_ones1"], R["Db"]
    cum_fm, bias_tok = R["cum_fm"], R["bias_tok"]
    xbc_ch = R["xbc_ch"]
    hstok = sb("hstok", [128, 2 * 8192], BF16)
    btok = sb("btok", [128, 2 * 1024], BF16)

    prev = sb("prev", [128, 8192], F32)
    sstb = sb("sstb", [128, 8192], F32)
    pwtmp = sb("pwtmp", [128, 512], F32)
    crow = [sb(f"crow{i}", [1, 2048], F32) for i in range(2)]
    clrow = sb("clrow", [1, 128], F32)
    cdrow = sb("cdrow", [1, 128], F32)
    sinb = sb("sinb", [128, 256], F32)
    sexp = sb("sexp", [128, 256], F32)
    decay = [sb(f"decay{i}", [128, 384], F32) for i in range(2)]
    msc = [sb(f"msc{i}", [128, 384], BF16) for i in range(2)]
    hssb = sb("hssb", [128, 128], BF16)
    dhsb = sb("dhsb", [128, 128], BF16)
    ypsb = [sb(f"ypsb{i}", [64, 256], BF16) for i in range(2)]

    def barrier(eng):
        for nm in S:
            p.wait(eng, s(nm), s(nm).n)

    barrier("vector"); barrier("scalar"); barrier("sync")
    p.run("vector", lambda e: e.memset(prev[:, :], 0.0), s("pw"))

    ps_free = [[], []]; decay_free = [[], []]; msc_free = [[], []]
    py_free = [[], []]; ypsb_free = [[], []]
    hss_free = []; dhs_free = []
    pdq_free = []; pst_free = []; crow_free = []
    clrow_free = []; cdrow_free = []; sstb_free = []
    xbcch_free = []; hstok_free = []; btok_free = []
    ptb_guard = []
    sexp_free = []

    def wmulti(eng, lst):
        for (nm, v) in lst:
            p.wait(eng, s(nm), v)

    for ch in range(NCH):
        p.wait("sync", s("stX"), 16 * 80)
        wmulti("sync", xbcch_free)
        for d5 in range(16):
            p.dma("sync", xbc_ch[:, d5 * 5 * 256:(d5 + 1) * 5 * 256],
                  xbcst[d5 * 5:(d5 + 1) * 5, :, ch * CS:(ch + 1) * CS]
                  .transpose([1, 0, 2]), s("rl"))
        v_rlx = s("rl").n
        with ExitStack() as esA:
            PTB = [esA.enter_context(nc.psum_tensor(f"PTB{ch}_{i}", [128, 128],
                                                    BF16)) for i in range(2)]
            ptbf = [list(ptb_guard), list(ptb_guard)]
            p.wait("tensor", s("rl"), v_rlx)
            wmulti("vector", hstok_free); wmulti("vector", btok_free)
            ntp = 0
            for tt2 in range(2):
                for cht in range(72):
                    par = ntp % 2
                    wmulti("tensor", ptbf[par])
                    src = (xbc_ch[:, cht * 256 + tt2 * 128: cht * 256 + tt2 * 128 + 128]
                           if cht < 64 else
                           xbc_ch[:, (cht) * 256 + tt2 * 128: (cht) * 256 + tt2 * 128 + 128])
                    p.run("tensor", lambda e, pr=par, sr=src: e.transpose(
                        PTB[pr][:, :], sr, c_identb[:, :]), s("ta"))
                    p.wait("vector", s("ta"), s("ta").n)
                    dst = (hstok[:, tt2 * 8192 + cht * 128: tt2 * 8192 + (cht + 1) * 128]
                           if cht < 64 else
                           btok[:, tt2 * 1024 + (cht - 64) * 128: tt2 * 1024 + (cht - 63) * 128])
                    p.run("vector", lambda e, pr=par, d=dst: e.tensor_copy(
                        d, PTB[pr][:, :]), s("ev2"))
                    ptbf[par] = [("ev2", s("ev2").n)]
                    ntp += 1
            v_tp_ev = s("ev2").n
        with ExitStack() as esB:
            psB = lambda nm, sh, dt=F32: esB.enter_context(
                nc.psum_tensor(f"{nm}_{ch}", sh, dt))
            PDQ = psB("PDQ", [128, 1536])
            PS = [psB(f"PS{i}", [128, 384]) for i in range(2)]
            PY = [psB(f"PY{i}", [128, 256]) for i in range(2)]
            PST = psB("PST", [128, 512])
            wmulti("tensor", pdq_free + pst_free)
            wmulti("tensor", clrow_free)
            p.wait("tensor", s("pw"), s("pw").n)
            p.wait("tensor", s("ev2"), v_cum)
            c0 = ch * CS + 255
            p.run("tensor", lambda e, cc=c0: e.matmul(
                PST[0:1, 0:128], cum_fm[:, cc:cc + 1], c_identf[:, :],
                start=True, stop=True), s("ta"))
            p.wait("vector", s("ta"), s("ta").n)
            p.run("vector", lambda e: e.tensor_copy(clrow[:, :],
                                                    PST[0:1, 0:128]), s("ev2"))
            v_clrow = s("ev2").n
            p.wait("sync", s("ev2"), v_clrow)
            p.dma("sync", O["clast"][ch:ch + 1, :], clrow[:, :], s("stO"))
            v_clast_dma = s("stO").n
            p.wait("tensor", s("ev2"), v_clrow)
            p.run("tensor", lambda e: e.matmul(PST[:, 0:128], c_ones1[:, :],
                                               clrow[:, :], start=True,
                                               stop=True), s("ta"))
            v_clb = s("ta").n
            p.wait("vector", s("ta"), v_clb)
            p.wait("vector", s("ev2"), v_biastok)
            wmulti("vector", sexp_free)
            for t2 in range(2):
                p.run("vector", lambda e, tt=t2, b0=(2 * ch + t2) * 128:
                      e.tensor_add(sinb[:, tt * 128:(tt + 1) * 128],
                                   PST[:, 0:128], bias_tok[:, b0:b0 + 128]),
                      s("ev2"))
            v_sinb = s("ev2").n
            p.wait("scalar", s("ev2"), v_sinb)
            p.run("scalar", lambda e: e.activation(sexp[:, :], sinb[:, :],
                                                   AF.Exp), s("ax"))
            v_sexp = s("ax").n
            if debug:
                p.wait("sync", s("ax"), v_sexp)
                p.dma("sync", O["dbg_sexp"][ch], sexp[:, :], s("stO"))
            for q in range(32):
                g16 = q // 4
                if q % 4 == 0:
                    wmulti("sync", crow_free)
                    for ib in range(2):
                        p.dma("sync", crow[ib][:, :],
                              cumrows[ch, ib,
                                      g16 * 2048:(g16 + 1) * 2048].unsqueeze(0),
                              s("rl"))
                v_crow = s("rl").n
                p.wait("tensor", s("rl"), v_crow)
                wmulti("tensor", pdq_free)
                qo = (q % 4) * 512
                p.run("tensor", lambda e, o=qo: e.matmul(
                    PDQ[:, 0:512], c_ones1[:, :], crow[0][:, o:o + 512],
                    start=True, stop=False))
                p.run("tensor", lambda e: e.matmul(
                    PDQ[:, 0:512], c_identf[:, :], c_maskd4[:, :],
                    start=False, stop=True))
                p.run("tensor", lambda e, o=qo: e.matmul(
                    PDQ[:, 512:1024], c_ones1[:, :], crow[1][:, o:o + 512],
                    start=True, stop=True))
                p.run("tensor", lambda e, o=qo: e.matmul(
                    PDQ[:, 1024:1536], c_ones1[:, :], crow[1][:, o:o + 512],
                    start=True, stop=False))
                p.run("tensor", lambda e: e.matmul(
                    PDQ[:, 1024:1536], c_identf[:, :], c_maskd4[:, :],
                    start=False, stop=True), s("dq"))
                v_dq = s("dq").n
                crow_free = [("dq", v_dq)]
                for hh in range(4):
                    h = q * 4 + hh
                    par = h % 2
                    g = h // 16
                    wmulti("tensor", ps_free[par])
                    bt = lambda jt, gg=g: xbc_ch[:, (64 + gg) * 256 + jt * 128:
                                                 (64 + gg) * 256 + (jt + 1) * 128]
                    ctl = lambda it, gg=g: xbc_ch[:, (72 + gg) * 256 + it * 128:
                                                  (72 + gg) * 256 + (it + 1) * 128]
                    for isl, (jt, it) in enumerate([(0, 0), (0, 1), (1, 1)]):
                        p.run("tensor", lambda e, pr=par, l=bt(jt), r=ctl(it),
                              s0=isl * 128: e.matmul(
                                  PS[pr][:, s0:s0 + 128], l, r,
                                  start=True, stop=True),
                              s("sc") if isl == 2 else None, 1)
                    v_sc = s("sc").n
                    wmulti("scalar", decay_free[par])
                    p.wait("scalar", s("dq"), v_dq)
                    bj0 = (2 * ch) * 128 + h
                    bj1 = (2 * ch + 1) * 128 + h
                    in01 = PDQ[:, :].rearrange("p (a f) -> p a f", f=128)[:, hh:hh + 5:4, :]
                    o01 = decay[par][:, 0:256].rearrange("p (a f) -> p a f", f=128)
                    p.run("scalar", lambda e, i_=in01, o_=o01,
                          b=bias_tok[:, bj0:bj0 + 1]:
                          e.activation(o_, i_, AF.Exp, bias=b))
                    in2 = PDQ[:, 1024 + hh * 128:1024 + (hh + 1) * 128]
                    p.run("scalar", lambda e, i_=in2, o_=decay[par][:, 256:384],
                          b=bias_tok[:, bj1:bj1 + 1]:
                          e.activation(o_, i_, AF.Exp, bias=b), s("ax"))
                    v_ax = s("ax").n
                    wmulti("vector", msc_free[par])
                    p.wait("vector", s("ax"), v_ax)
                    p.wait("vector", s("sc"), v_sc)
                    p.run("vector", lambda e, pr=par: e.tensor_mul(
                        msc[pr][:, :], PS[pr][:, :], decay[pr][:, :]), s("ms"))
                    v_ms = s("ms").n
                    ps_free[par] = [("ms", v_ms)]
                    decay_free[par] = [("ms", v_ms)]
                    p.wait("vector", s("ax"), v_sexp)
                    wmulti("vector", hss_free); wmulti("vector", dhs_free)
                    for t2 in range(2):
                        hsl = hstok[:, t2 * 8192 + h * 64: t2 * 8192 + (h + 1) * 64]
                        p.run("vector", lambda e, o=hssb[:, t2 * 64:(t2 + 1) * 64],
                              i_=hsl, sc_=sexp[:, t2 * 128 + h: t2 * 128 + h + 1]:
                              e.tensor_scalar_mul(o, i_, sc_))
                        p.run("vector", lambda e, o=dhsb[:, t2 * 64:(t2 + 1) * 64],
                              i_=hsl, sc_=Db[:, h:h + 1]:
                              e.tensor_scalar_mul(o, i_, sc_), s("ev2"))
                    v_hd = s("ev2").n
                    p.wait("tensor", s("ev2"), v_hd)
                    if h % 8 == 0:
                        wmulti("tensor", pst_free)
                        if h == 0:
                            p.wait("tensor", s("ev2"), v_sinb)
                    for t2 in range(2):
                        p.run("tensor", lambda e, tt=t2, gg=g, o=(h % 8) * 64:
                              e.matmul(PST[:, o:o + 64],
                                       btok[:, tt * 1024 + gg * 128:
                                            tt * 1024 + (gg + 1) * 128],
                                       hssb[:, tt * 64:(tt + 1) * 64],
                                       start=(tt == 0), stop=(tt == 1)),
                              s("sm") if t2 == 1 else None, 1)
                    v_sm = s("sm").n
                    hss_free = [("sm", v_sm)]
                    p.wait("tensor", s("ms"), v_ms)
                    wmulti("tensor", py_free[par])
                    hs0 = hstok[:, 0 * 8192 + h * 64: 0 * 8192 + (h + 1) * 64]
                    hs1 = hstok[:, 1 * 8192 + h * 64: 1 * 8192 + (h + 1) * 64]
                    pb = 0
                    pyp = PY[par]
                    p.run("tensor", lambda e, l=hs0, pr=par, py=pyp: e.matmul(
                        py[0:64, 0:128], l, msc[pr][:, 0:128],
                        start=True, stop=False))
                    p.run("tensor", lambda e, py=pyp: e.matmul(
                        py[0:64, 0:128], dhsb[:, 0:64], c_identb[:, :],
                        start=False, stop=True))
                    p.run("tensor", lambda e, l=hs0, pr=par, py=pyp: e.matmul(
                        py[0:64, 128:256], l, msc[pr][:, 128:256],
                        start=True, stop=False))
                    p.run("tensor", lambda e, l=hs1, pr=par, py=pyp: e.matmul(
                        py[0:64, 128:256], l, msc[pr][:, 256:384],
                        start=False, stop=False))
                    p.run("tensor", lambda e, py=pyp: e.matmul(
                        py[0:64, 128:256], dhsb[:, 64:128], c_identb[:, :],
                        start=False, stop=True), s("ym"))
                    v_ym = s("ym").n
                    msc_free[par] = [("ym", v_ym)]
                    dhs_free = [("ym", v_ym)]
                    if debug and ch == 0:
                        p.wait("sync", s("ms"), v_ms)
                        p.dma("sync", O["dbg_msc"][h], msc[par][:, :], s("stO"))
                        msc_free[par].append(("stO", s("stO").n))
                    p.wait("vector", s("ym"), v_ym)
                    wmulti("vector", ypsb_free[par])
                    p.run("vector", lambda e, o=ypsb[par][0:64, :], pr=par:
                          e.tensor_copy(o, PY[pr][0:64, :]), s("ye"))
                    py_free[par] = [("ye", s("ye").n)]
                    p.wait("sync", s("ye"), s("ye").n)
                    p.dma("sync", O["yp"][h * 64:(h + 1) * 64,
                                          ch * CS:(ch + 1) * CS],
                          ypsb[par][0:64, :], s("stO"))
                    ypsb_free[par] = [("stO", s("stO").n)]
                    if h % 8 == 7:
                        p.wait("vector", s("sm"), v_sm)
                        wmulti("vector", sstb_free)
                        q8 = h // 8
                        p.run("vector", lambda e, qq=q8: e.tensor_copy(
                            sstb[:, :].rearrange("n (pp hh) -> n hh pp", hh=128)
                            [:, qq * 8:(qq + 1) * 8, :],
                            PST[:, :].rearrange("n (hh pp) -> n hh pp", pp=64)),
                            s("se"))
                        pst_free = [("se", s("se").n)]
                pdq_free = [("ax", s("ax").n)]
            v_se_all = s("se").n
            p.wait("sync", s("se"), v_se_all)
            p.dma("sync", O["sst"][ch], sstb[:, :], s("so"))
            v_sst_dma = s("so").n
            wmulti("scalar", cdrow_free)
            p.wait("scalar", s("ev2"), v_clrow)
            p.run("scalar", lambda e: e.activation(cdrow[:, :], clrow[:, :],
                                                   AF.Exp), s("ax"))
            v_cd = s("ax").n
            p.wait("tensor", s("ax"), v_cd)
            wmulti("tensor", pst_free)
            p.run("tensor", lambda e: e.matmul(
                PST[:, :], c_ones1[:, :],
                cdrow[:, :].unsqueeze(1).broadcast_to([1, 4, 128]),
                start=True, stop=True), s("ta"))
            v_wb = s("ta").n
            cdrow_free = [("ta", v_wb)]
            clrow_free = [("stO", v_clast_dma), ("ta", v_clb), ("ax", v_cd)]
            p.wait("vector", s("ta"), v_wb)
            p.wait("vector", s("se"), v_se_all)
            for m in range(16):
                p.run("vector", lambda e, o=m * 512: e.tensor_mul(
                    pwtmp[:, :], prev[:, o:o + 512], PST[:, :]))
                p.run("vector", lambda e, o=m * 512: e.tensor_add(
                    prev[:, o:o + 512], pwtmp[:, :], sstb[:, o:o + 512]),
                    s("pw") if m == 15 else None, 1)
            v_pw = s("pw").n
            pst_free = [("pw", v_pw)]
            sstb_free = [("so", v_sst_dma), ("pw", v_pw)]
            sexp_free = [("ev2", v_hd)]
            xbcch_free = [("sc", s("sc").n), ("ta", v_wb)]
            hstok_free = [("ym", s("ym").n), ("ev2", v_hd)]
            btok_free = [("sm", s("sm").n)]
            ptb_guard = [("ax", s("ax").n), ("ms", s("ms").n),
                         ("ye", s("ye").n), ("se", s("se").n), ("pw", v_pw)]
            pdq_free = [("ax", s("ax").n)]
    p.wait("sync", s("pw"), s("pw").n)
    p.dma("sync", O["score"][:, :], prev[:, :], s("so"))
    for nm in ("stX", "stO", "so", "rl", "cld"):
        p.wait("sync", s(nm), s(nm).n)
    p.wait("gpsimd", s("wld"), s("wld").n)
    p.wait("gpsimd", s("xld"), s("xld").n)
    p.finalize()
    es.close()
    return nc


def stage_l2_relay(r1_list):
    per = []
    for k in range(8):
        sp = np.zeros((7, 128, 8192), np.float32)
        cp = np.full((7, 4, 128), -1e4, np.float32)
        for a in range(7):
            j = k - 1 - a
            if j >= 0:
                sp[a] = r1_list[j]["score"]
                cp[a] = r1_list[j]["clast"]
        per.append(dict(
            yp=r1_list[k]["yp"], sg=r1_list[k]["sg"],
            sstin=r1_list[k]["sst"], cumin=r1_list[k]["cum"],
            clastin=r1_list[k]["clast"], cfmin=r1_list[k]["cfm"],
            spred=sp, clpred=cp))
    return per


def build_l2(debug=False):
    nc = bass.Bass("TRN2", target_bir_lowering=False, debug=False)
    es = ExitStack()
    p = Prog(nc, es)
    I = {}
    for nm, sh, dt in [
        ("yp", [HS, T], BF16), ("sg", [GATE, T], BF16),
        ("sstin", [NCH, 128, 8192], F32), ("cumin", [128, T], F32),
        ("clastin", [NCH, 128], F32), ("cfmin", [8, 128, T], BF16),
        ("spred", [7, 128, 8192], F32), ("clpred", [7, 4, 128], F32),
        ("x0tok", [T, DM], F32),
        ("wop", [8, 64, 128, 512], F32), ("wbase", [8, 32, 128, 512], F32),
        ("wdown", [32, 128, 4, 128], F32), ("wqT", [32, 128, 128], F32),
        ("normw", [128, 64], F32), ("identf", [128, 128], F32),
        ("identb", [128, 128], BF16), ("onesrow", [1, 128], F32),
        ("tri7", [7, 7], F32),
    ]:
        I[nm] = nc.dram_tensor(nm, sh, dt, kind="ExternalInput").ap()
    O = {}
    outs = [("x1", [T, DM], F32), ("basew", [T, DM], F32),
            ("hmat", [512, T], BF16), ("qp", [1, 128], F32)]
    if debug:
        outs += [("dbg_init", [128, 8192], F32), ("dbg_g", [64, 128, T], BF16),
                 ("dbg_attn", [T, DM], F32), ("dbg_ghf", [128, 64 * 512], BF16)]
    for nm, sh, dt in outs:
        O[nm] = nc.dram_tensor(nm, sh, dt, kind="ExternalOutput").ap()
    cumrows2 = nc.dram_tensor("cumrows2", [NCH, 2, 128 * 128], F32, kind="Internal").ap()
    gst = nc.dram_tensor("gst", [64, 128, T], BF16, kind="Internal").ap()
    wrows = nc.dram_tensor("wrows", [7, 128], F32, kind="Internal").ap()
    x1i = nc.dram_tensor("x1i", [T, DM], F32, kind="Internal").ap()

    sb = lambda nm, sh, dt: es.enter_context(nc.sbuf_tensor(nm, sh, dt))
    S = {nm: p.sem(nm) for nm in
         ("cld wld rl mm ta ev2 ax st so pw dq ms ym ye gq").split()}
    s = lambda nm: S[nm]

    def wmulti(eng, lst):
        for (nm, v) in lst:
            p.wait(eng, s(nm), v)

    c_identf = sb("c_identf", [128, 128], F32)
    c_identb = sb("c_identb", [128, 128], BF16)
    c_ones1 = sb("c_ones1", [1, 128], F32)
    c_onesbc = sb("c_onesbc", [128, 1], BF16)
    c_normw = sb("c_normw", [128, 64], F32)
    c_eps = sb("c_eps", [128, 1], F32)
    esP = ExitStack()
    sbP = lambda nm, sh, dt: esP.enter_context(nc.sbuf_tensor(nm, sh, dt))
    c_tri7 = sbP("c_tri7", [7, 7], F32)
    cfm_sb = sbP("cfm_sb", [128, 8 * T], BF16)
    cl_sb = sbP("cl_sb", [1, 512], F32)
    pv = sbP("pv", [128, 8192], F32)
    pvb = sbP("pvb", [128, 8192], BF16)
    spbuf = sbP("spbuf", [128, 8192], F32)
    pwtmp = sbP("pwtmp", [128, 512], F32)
    wrow_sb = sbP("wrow_sb", [1, 896], F32)
    crow = [sbP(f"crow{i}", [1, 2048], F32) for i in range(2)]
    cdr = sbP("cdr", [1, 128], F32)

    for nm, dst in [("identf", c_identf), ("identb", c_identb),
                    ("onesrow", c_ones1), ("tri7", c_tri7),
                    ("normw", c_normw)]:
        p.dma("sync", dst[:, :], I[nm][:, :], s("cld"))
    p.dma("sync", cl_sb[:, :], I["clastin"].rearrange("c h -> (c h)").unsqueeze(0),
          s("cld"))
    for h2 in range(2):
        p.dma("sync", cfm_sb[:, h2 * 4096:(h2 + 1) * 4096],
              I["cfmin"][h2 * 4:(h2 + 1) * 4].transpose([1, 0, 2]), s("cld"))
    for c in range(NCH):
        p.dma("sync", cumrows2[c].rearrange("b (h i) -> h b i", i=128),
              I["cumin"][:, c * CS:(c + 1) * CS].rearrange("h (b i) -> h b i", b=2),
              s("rl"))
    v_cld = s("cld").n
    p.run("vector", lambda e: e.memset(c_onesbc[:, :], 1.0))
    p.run("vector", lambda e: e.memset(c_eps[:, :], EPS), s("ev2"))
    clp = sbP("clp", [7, 512], F32)
    dsc = sbP("dsc", [7, 128], F32)
    dsc2 = sbP("dsc2", [7, 128], F32)
    wsb = sbP("wsb", [7, 128], F32)
    p.dma("sync", clp[:, :], I["clpred"].rearrange("a c h -> a (c h)"), s("cld"))
    p.wait("vector", s("cld"), s("cld").n)
    p.run("vector", lambda e: e.tensor_add(dsc[:, :], clp[:, 0:128],
                                           clp[:, 128:256]))
    p.run("vector", lambda e: e.tensor_add(dsc2[:, :], clp[:, 256:384],
                                           clp[:, 384:512]))
    p.run("vector", lambda e: e.tensor_add(dsc[:, :], dsc[:, :], dsc2[:, :]),
          s("ev2"))
    with ExitStack() as es0:
        PW = es0.enter_context(nc.psum_tensor("PW", [7, 128], F32))
        p.wait("tensor", s("cld"), v_cld)
        p.wait("tensor", s("ev2"), s("ev2").n)
        p.run("tensor", lambda e: e.matmul(PW[:, :], c_tri7[:, :], dsc[:, :],
                                           start=True, stop=True), s("ta"))
        p.wait("scalar", s("ta"), s("ta").n)
        p.run("scalar", lambda e: e.activation(wsb[:, :], PW[:, :], AF.Exp),
              s("ax"))
        p.wait("sync", s("ax"), s("ax").n)
        p.dma("sync", wrows[:, :], wsb[:, :], s("so"))
        p.wait("sync", s("so"), s("so").n)
        p.dma("sync", wrow_sb[:, 0:896],
              wrows.rearrange("a h -> (a h)").unsqueeze(0), s("rl"))
    v_wrow = s("rl").n
    p.run("vector", lambda e: e.memset(pv[:, :], 0.0), s("pw"))
    with ExitStack() as esI:
        PB2 = esI.enter_context(nc.psum_tensor("PB2i", [128, 512], F32))
        for a in range(7):
            p.wait("sync", s("pw"), s("pw").n)
            p.dma("sync", spbuf[:, :], I["spred"][a].rearrange("p f -> p f"),
                  s("rl"))
            v_sp = s("rl").n
            p.wait("tensor", s("rl"), v_wrow)
            p.wait("tensor", s("pw"), s("pw").n)
            p.run("tensor", lambda e, aa=a: e.matmul(
                PB2[:, :], c_ones1[:, :],
                wrow_sb[:, aa * 128:(aa + 1) * 128].unsqueeze(1)
                .broadcast_to([1, 4, 128]), start=True, stop=True), s("ta"))
            p.wait("vector", s("ta"), s("ta").n)
            p.wait("vector", s("rl"), v_sp)
            for m in range(16):
                p.run("vector", lambda e, o=m * 512: e.tensor_mul(
                    pwtmp[:, :], spbuf[:, o:o + 512], PB2[:, :]))
                p.run("vector", lambda e, o=m * 512: e.tensor_add(
                    pv[:, o:o + 512], pv[:, o:o + 512], pwtmp[:, :]),
                    s("pw") if m == 15 else None, 1)
    v_init = s("pw").n
    if debug:
        p.wait("sync", s("pw"), v_init)
        p.dma("sync", O["dbg_init"][:, :], pv[:, :], s("so"))
    p.finalize_marker = None
    return _l2_tail(nc, es, p, I, O, cumrows2, gst, x1i, sb, S, debug,
                    dict(c_identf=c_identf, c_identb=c_identb,
                         c_ones1=c_ones1, c_onesbc=c_onesbc, c_normw=c_normw,
                         cfm_sb=cfm_sb, cl_sb=cl_sb, pv=pv, pvb=pvb,
                         c_eps=c_eps,
                         spbuf=spbuf, pwtmp=pwtmp, crow=crow, cdr=cdr,
                         esP=esP),
                    v_init)


def _l2_tail(nc, es, p, I, O, cumrows2, gst, x1i, sb, S, debug, R, v_init):
    s = lambda nm: S[nm]
    c_identf, c_identb = R["c_identf"], R["c_identb"]
    c_ones1, c_onesbc, c_normw = R["c_ones1"], R["c_onesbc"], R["c_normw"]
    cfm_sb, cl_sb = R["cfm_sb"], R["cl_sb"]
    pv, pvb, spbuf, pwtmp = R["pv"], R["pvb"], R["spbuf"], R["pwtmp"]
    crow, cdr = R["crow"], R["cdr"]

    def wmulti(eng, lst):
        for (nm, v) in lst:
            p.wait(eng, s(nm), v)

    esY = ExitStack()
    sbY = lambda nm, sh, dt: esY.enter_context(nc.sbuf_tensor(nm, sh, dt))
    ypch = sbY("ypch", [128, 64 * 256], BF16)
    sgch = sbY("sgch", [128, 64 * 256], BF16)
    ecb = [sbY(f"ecb{i}", [128, 256], BF16) for i in range(2)]
    ctb = [sbY(f"ctb{i}", [128, 256], BF16) for i in range(2)]
    ts1 = sbY("ts1", [128, 256], F32)
    ggrp = sbY("ggrp", [128, 8 * 256], BF16)
    sqb = sbY("sqb", [128, 256], BF16)
    rsb = sbY("rsb", [1, 256], F32)
    gfm = [sbY(f"gfm{i}", [128, 256], BF16) for i in range(2)]
    esYp = ExitStack()
    PU = esYp.enter_context(nc.psum_tensor("PU", [128, 1024], F32))
    PYI = [esYp.enter_context(nc.psum_tensor(f"PYI{i}", [128, 256], F32))
           for i in range(2)]
    PG = esYp.enter_context(nc.psum_tensor("PG", [1, 256], F32))
    PB2 = esYp.enter_context(nc.psum_tensor("PB2", [128, 512], F32))

    ecb_free = [[], []]; ctb_free = [[], []]; pyi_free = [[], []]
    pu_free = []; crow_free = []; gfm_free = [[], []]
    pg_free = []; pb2_free = []
    v_pvb = 0

    for ch in range(NCH):
        p.wait("vector", s("pw"), s("pw").n)
        wmulti("vector", [("ym", s("ym").n), ("ax", s("ax").n),
                          ("ye", s("ye").n)])
        p.run("vector", lambda e: e.tensor_copy(pvb[:, :], pv[:, :]), s("ev2"))
        v_pvb = s("ev2").n
        wmulti("sync", [("ym", s("ym").n), ("ms", s("ms").n)])
        for d8 in range(8):
            p.dma("sync", ypch[:, d8 * 8 * 256:(d8 + 1) * 8 * 256],
                  I["yp"].rearrange("(a p) t -> a p t", p=128)
                  [d8 * 8:(d8 + 1) * 8, :, ch * CS:(ch + 1) * CS]
                  .transpose([1, 0, 2]), s("rl"))
            p.dma("sync", sgch[:, d8 * 8 * 256:(d8 + 1) * 8 * 256],
                  I["sg"].rearrange("(a p) t -> a p t", p=128)
                  [d8 * 8:(d8 + 1) * 8, :, ch * CS:(ch + 1) * CS]
                  .transpose([1, 0, 2]), s("rl"))
        v_rly = s("rl").n
        p.wait("vector", s("rl"), v_rly)
        for q in range(32):
            g16 = q // 4
            if q % 4 == 0:
                wmulti("sync", crow_free)
                for ib in range(2):
                    p.dma("sync", crow[ib][:, :],
                          cumrows2[ch, ib, g16 * 2048:(g16 + 1) * 2048]
                          .unsqueeze(0), s("rl"))
            v_crow = s("rl").n
            p.wait("tensor", s("rl"), v_crow)
            wmulti("tensor", pu_free)
            qo = (q % 4) * 512
            p.run("tensor", lambda e, o=qo: e.matmul(
                PU[:, 0:512], c_ones1[:, :], crow[0][:, o:o + 512],
                start=True, stop=True))
            p.run("tensor", lambda e, o=qo: e.matmul(
                PU[:, 512:1024], c_ones1[:, :], crow[1][:, o:o + 512],
                start=True, stop=True), s("dq"))
            v_dq = s("dq").n
            crow_free = [("dq", v_dq)]
            for hh in range(4):
                h = q * 4 + hh
                par = h % 2
                g = h // 16
                cht = h // 2
                b = (h % 2) * 64
                wmulti("scalar", ecb_free[par])
                p.wait("scalar", s("dq"), v_dq)
                inu = PU[:, :].rearrange("p (a f) -> p a f", f=128)[:, hh:hh + 5:4, :]
                p.run("scalar", lambda e, i_=inu, o_=ecb[par][:, :]
                      .rearrange("p (a f) -> p a f", f=128):
                      e.activation(o_, i_, AF.Exp), s("ax"))
                v_ax = s("ax").n
                wmulti("vector", ctb_free[par])
                p.wait("vector", s("ax"), v_ax)
                cslice = cfm_sb[:, g * T + ch * CS: g * T + (ch + 1) * CS]
                p.run("vector", lambda e, pr=par, cs=cslice: e.tensor_mul(
                    ctb[pr][:, :], cs, ecb[pr][:, :]), s("ms"))
                v_ms = s("ms").n
                ecb_free[par] = [("ym", 0), ("ms", v_ms)]
                p.wait("tensor", s("ms"), v_ms)
                p.wait("tensor", s("ev2"), v_pvb)
                wmulti("tensor", pyi_free[par])
                lhs = pvb[:, :].rearrange("n (pp hh) -> n hh pp", hh=128)[:, h, :]
                if b:
                    p.run("tensor", lambda e, l=lhs, pr=par: e.matmul(
                        PYI[pr][64:128, :], l, ctb[pr][:, :],
                        start=True, stop=True, tile_position=(0, 64)), s("ym"))
                else:
                    p.run("tensor", lambda e, l=lhs, pr=par: e.matmul(
                        PYI[pr][0:64, :], l, ctb[pr][:, :],
                        start=True, stop=True), s("ym"))
                v_ym = s("ym").n
                ctb_free[par] = [("ym", v_ym)]
                p.wait("vector", s("ym"), v_ym)
                ypsl = ypch[b:b + 64, cht * 256:(cht + 1) * 256]
                sgsl = sgch[b:b + 64, cht * 256:(cht + 1) * 256]
                ggsl = ggrp[b:b + 64, (cht % 8) * 256:(cht % 8 + 1) * 256]
                p.run("vector", lambda e, pr=par, yy=ypsl, bb=b: e.tensor_add(
                    ts1[bb:bb + 64, :], PYI[pr][bb:bb + 64, :], yy))
                p.run("vector", lambda e, gg=ggsl, ss=sgsl, bb=b: e.tensor_mul(
                    gg, ts1[bb:bb + 64, :], ss), s("ms"))
                pyi_free[par] = [("ms", s("ms").n)]
                if h % 2 == 1:
                    p.wait("scalar", s("ms"), s("ms").n)
                    p.run("scalar", lambda e, gg=ggrp[:, (cht % 8) * 256:
                                                      (cht % 8 + 1) * 256]:
                          e.activation(sqb[:, :], gg, AF.Square), s("ax"))
                    p.wait("tensor", s("ax"), s("ax").n)
                    if cht % 8 == 0:
                        wmulti("tensor", pg_free)
                    p.run("tensor", lambda e, st=(cht % 8 == 0): e.matmul(
                        PG[:, :], c_onesbc[:, :], sqb[:, :],
                        start=st, stop=(cht % 8 == 7)), s("gq"))
                    if cht % 8 == 7:
                        v_gq = s("gq").n
                        p.wait("scalar", s("gq"), v_gq)
                        p.run("scalar", lambda e: e.activation(
                            rsb[:, :], PG[:, :], AF.Ln, scale=1.0 / 1024,
                            bias=R["c_eps"][0:1, :]))
                        p.run("scalar", lambda e: e.activation(
                            rsb[:, :], rsb[:, :], AF.Exp, scale=-0.5), s("ax"))
                        v_rstd = s("ax").n
                        pg_free = [("ax", v_rstd)]
                        p.wait("tensor", s("ax"), v_rstd)
                        wmulti("tensor", pb2_free)
                        p.run("tensor", lambda e: e.matmul(
                            PB2[:, 0:256], c_ones1[:, :], rsb[:, :],
                            start=True, stop=True), s("ta"))
                        v_pb2 = s("ta").n
                        p.wait("vector", s("ta"), v_pb2)
                        for t8 in range(8):
                            chtt = (cht // 8) * 8 + t8
                            parg = t8 % 2
                            wmulti("vector", gfm_free[parg])
                            p.run("vector", lambda e, gg=ggrp[:, t8 * 256:
                                                              (t8 + 1) * 256],
                                  w=c_normw[:, chtt:chtt + 1], pr=parg:
                                  e.scalar_tensor_tensor(
                                      gfm[pr][:, :], gg, w, PB2[:, 0:256],
                                      ALU.mult, ALU.mult), s("ye"))
                            v_gf = s("ye").n
                            p.wait("sync", s("ye"), v_gf)
                            p.dma("sync", gst[chtt][:, ch * CS:(ch + 1) * CS],
                                  gfm[parg][:, :], s("st"))
                            gfm_free[parg] = [("st", s("st").n)]
                            if debug:
                                p.dma("sync", O["dbg_g"][chtt]
                                      [:, ch * CS:(ch + 1) * CS],
                                      gfm[parg][:, :], s("st"))
                                gfm_free[parg].append(("st", s("st").n))
                        pb2_free = [("ye", s("ye").n)]
            pu_free = [("ax", s("ax").n)]
        if ch < NCH - 1:
            p.wait("sync", s("pw"), s("pw").n)
            p.dma("sync", spbuf[:, :], I["sstin"][ch], s("rl"))
            v_sst = s("rl").n
            p.wait("scalar", s("cld"), s("cld").n)
            p.run("scalar", lambda e, c0=ch * 128: e.activation(
                cdr[:, :], cl_sb[0:1, c0:c0 + 128], AF.Exp), s("ax"))
            v_cd = s("ax").n
            p.wait("tensor", s("ax"), v_cd)
            wmulti("tensor", pb2_free)
            p.run("tensor", lambda e: e.matmul(
                PB2[:, :], c_ones1[:, :],
                cdr[:, :].unsqueeze(1).broadcast_to([1, 4, 128]),
                start=True, stop=True), s("ta"))
            v_wb = s("ta").n
            p.wait("vector", s("ta"), v_wb)
            p.wait("vector", s("rl"), v_sst)
            p.wait("vector", s("ym"), s("ym").n)
            for m in range(16):
                p.run("vector", lambda e, o=m * 512: e.tensor_mul(
                    pwtmp[:, :], pv[:, o:o + 512], PB2[:, :]))
                p.run("vector", lambda e, o=m * 512: e.tensor_add(
                    pv[:, o:o + 512], pwtmp[:, :], spbuf[:, o:o + 512]),
                    s("pw") if m == 15 else None, 1)
            pb2_free = [("pw", s("pw").n)]
    v_y_done = dict(st=s("st").n, ye=s("ye").n, ms=s("ms").n, ym=s("ym").n,
                    ax=s("ax").n, rl=s("rl").n)
    esYp.close()
    esY.close()
    R["esP"].close()
    return _l2_outproj(nc, es, p, I, O, gst, x1i, sb, S, debug, R, v_y_done)


def _l2_outproj(nc, es, p, I, O, gst, x1i, sb, S, debug, R, v_y):
    s = lambda nm: S[nm]
    c_identf, c_ones1, c_onesbc = R["c_identf"], R["c_ones1"], R["c_onesbc"]

    def wmulti(eng, lst):
        for (nm, v) in lst:
            p.wait(eng, s(nm), v)

    es2 = ExitStack()
    sb2 = lambda nm, sh, dt: es2.enter_context(nc.sbuf_tensor(nm, sh, dt))
    ghf = sb2("ghf", [128, 64 * 512], BF16)
    wopb = [sb2(f"wopb{i}", [128, 16 * 512], BF16) for i in range(2)]
    x2a = sb2("x2a", [128, 4 * 4096], F32)
    x0t = sb2("x0t", [128, 4096], F32)
    lnt = sb2("lnt", [128, 4096], F32)
    mcol = sb2("mcol", [128, 1], F32)
    mcol2 = sb2("mcol2", [128, 1], F32)
    vcol = sb2("vcol", [128, 1], F32)
    es2p = ExitStack()
    PO = [es2p.enter_context(nc.psum_tensor(f"PO{i}", [128, 512], F32))
          for i in range(8)]
    for eng in ("vector", "scalar", "sync", "tensor", "gpsimd"):
        for nm in S:
            p.wait(eng, s(nm), s(nm).n)
    wop_free = [[], []]
    po_free = [[] for _ in range(8)]
    ghf_free = []
    x2a_free = []
    for th in range(2):
        wmulti("sync", ghf_free)
        for c8 in range(8):
            p.dma("sync", ghf[:, c8 * 8 * 512:(c8 + 1) * 8 * 512],
                  gst[c8 * 8:(c8 + 1) * 8, :, th * 512:(th + 1) * 512]
                  .transpose([1, 0, 2]), s("rl"))
        v_g = s("rl").n
        if debug and th == 0:
            p.wait("sync", s("rl"), v_g)
            p.dma("sync", O["dbg_ghf"][:, :], ghf[:, :], s("so"))
        for dc in range(8):
            pp = dc % 2
            for kg in range(4):
                wmulti("gpsimd", wop_free[kg % 2])
                p.dma("gpsimd",
                      wopb[kg % 2][:, :].rearrange("p (a j) -> p a j", a=16),
                      I["wop"][dc, kg * 16:(kg + 1) * 16]
                      .transpose([1, 0, 2]), s("wld"))
                v_w = s("wld").n
                p.wait("tensor", s("wld"), v_w)
                if kg == 0:
                    p.wait("tensor", s("rl"), v_g)
                    for t4 in range(4):
                        wmulti("tensor", po_free[pp * 4 + t4])
                for t4 in range(4):
                    for k in range(16):
                        kk = kg * 16 + k
                        lhs = ghf[:, kk * 512 + t4 * 128: kk * 512 + (t4 + 1) * 128]
                        p.run("tensor", lambda e, l=lhs, r=wopb[kg % 2]
                              [:, k * 512:(k + 1) * 512],
                              po=PO[pp * 4 + t4][:, :],
                              st=(kg == 0 and k == 0), sp=(kg == 3 and k == 15):
                              e.matmul(po, l, r, start=st, stop=sp),
                              s("mm") if (k == 15 and t4 == 3)
                              else None, 1)
                wop_free[kg % 2] = [("mm", s("mm").n)]
            v_mm = s("mm").n
            p.wait("vector", s("mm"), v_mm)
            wmulti("vector", x2a_free)
            for t4 in range(4):
                p.run("vector", lambda e, po=PO[pp * 4 + t4][:, :],
                      o=x2a[:, t4 * 4096 + dc * 512: t4 * 4096 + (dc + 1) * 512]:
                      e.tensor_copy(o, po), s("ev2"))
                po_free[pp * 4 + t4] = [("ev2", s("ev2").n)]
                if debug:
                    p.wait("sync", s("ev2"), s("ev2").n)
                    p.dma("sync", O["dbg_attn"]
                          [th * 512 + t4 * 128: th * 512 + (t4 + 1) * 128,
                           dc * 512:(dc + 1) * 512],
                          x2a[:, t4 * 4096 + dc * 512: t4 * 4096 + (dc + 1) * 512],
                          s("so"))
        v_evall = s("ev2").n
        x2a_free = []
        for t4 in range(4):
            row0 = th * 512 + t4 * 128
            p.wait("sync", s("gq"), s("gq").n)
            p.dma("sync", x0t[:, :], I["x0tok"][row0:row0 + 128, :], s("rl"))
            v_x0 = s("rl").n
            p.wait("vector", s("rl"), v_x0)
            p.wait("vector", s("ev2"), v_evall)
            p.run("vector", lambda e, a=x2a[:, t4 * 4096:(t4 + 1) * 4096]:
                  e.tensor_add(lnt[:, :], a, x0t[:, :]))
            p.run("vector", lambda e: e.reduce_sum(mcol[:, :], lnt[:, :],
                                                   axis=AX.X))
            p.run("vector", lambda e: e.memset(vcol[:, :], 0.0), s("ev2"))
            p.wait("scalar", s("ev2"), s("ev2").n)
            p.run("scalar", lambda e: e.activation(mcol2[:, :], mcol[:, :],
                                                   AF.Identity, scale=1.0 / DM),
                  s("ax"))
            p.wait("vector", s("ax"), s("ax").n)
            p.run("vector", lambda e, a=x2a[:, t4 * 4096:(t4 + 1) * 4096]:
                  e.tensor_scalar(a, lnt[:, :], mcol2[:, :], None,
                                  ALU.subtract), s("ev2"))
            x2a_free.append(("ev2", s("ev2").n))
            p.wait("scalar", s("ev2"), s("ev2").n)
            p.run("scalar", lambda e, a=x2a[:, t4 * 4096:(t4 + 1) * 4096]:
                  e.activation(ghf[:, 0:8192].bitcast(F32), a, AF.Square,
                               accum_out=vcol[:, :]))
            p.run("scalar", lambda e: e.activation(vcol[:, :], vcol[:, :],
                                                   AF.Ln, scale=1.0 / DM,
                                                   bias=R["c_eps"][:, :]))
            p.run("scalar", lambda e: e.activation(vcol[:, :], vcol[:, :],
                                                   AF.Exp, scale=-0.5), s("ax"))
            p.wait("vector", s("ax"), s("ax").n)
            p.run("vector", lambda e, a=x2a[:, t4 * 4096:(t4 + 1) * 4096]:
                  e.tensor_scalar_mul(lnt[:, :], a, vcol[:, :]), s("ev2"))
            v_x1 = s("ev2").n
            p.wait("sync", s("ev2"), v_x1)
            p.dma("sync", O["x1"][row0:row0 + 128, :], lnt[:, :], s("so"))
            p.dma("sync", x1i[row0:row0 + 128, :], lnt[:, :], s("st"))
            x2a_free.append(("so", s("so").n))
            x2a_free.append(("ax", s("ax").n))
            p.run("sync", lambda e: e.nop(), s("gq"), 1)
        ghf_free = [("mm", s("mm").n), ("ax", s("ax").n)]
    v_x1_all = s("st").n
    es2p.close()
    es2.close()
    return _l2_ffn(nc, es, p, I, O, x1i, sb, S, debug, R, v_x1_all)


def _l2_ffn(nc, es, p, I, O, x1i, sb, S, debug, R, v_x1_all):
    s = lambda nm: S[nm]
    c_identf, c_onesbc = R["c_identf"], R["c_onesbc"]

    def wmulti(eng, lst):
        for (nm, v) in lst:
            p.wait(eng, s(nm), v)

    es3 = ExitStack()
    sb3 = lambda nm, sh, dt: es3.enter_context(nc.sbuf_tensor(nm, sh, dt))
    x1fm = sb3("x1fm", [128, 32 * 1024], BF16)
    x1t = sb3("x1t", [128, 4096], F32)
    wb2 = sb3("wb2", [128, 32 * 512], BF16)
    wq_sb = sb3("wq_sb", [128, 32 * 128], BF16)
    wd_sb = sb3("wd_sb", [128, 32 * 512], BF16)
    bout = [sb3(f"bout{i}", [128, 512], F32) for i in range(2)]
    qxt = sb3("qxt", [128, 128], BF16)
    qrow = sb3("qrow", [1, 128], F32)
    hout = [sb3(f"hout{i}", [128, 512], BF16) for i in range(2)]
    es3p = ExitStack()
    PTF = [es3p.enter_context(nc.psum_tensor(f"PTF{i}", [128, 128], F32))
           for i in range(2)]
    POB = [es3p.enter_context(nc.psum_tensor(f"POB{i}", [128, 512], F32))
           for i in range(2)]
    PQX = es3p.enter_context(nc.psum_tensor("PQX", [128, 128], F32))
    PQ = es3p.enter_context(nc.psum_tensor("PQ", [1, 128], F32))
    PH2_ = [es3p.enter_context(nc.psum_tensor(f"PH{i}", [128, 512], F32))
            for i in range(2)]
    for eng in ("vector", "scalar", "sync", "tensor", "gpsimd"):
        for nm in S:
            p.wait(eng, s(nm), s(nm).n)
    p.dma("gpsimd", wq_sb[:, :].rearrange("p (a r) -> p a r", a=32),
          I["wqT"].transpose([1, 0, 2]), s("wld"))
    p.dma("gpsimd", wd_sb[:, :].rearrange("p (a b r) -> p a b r", a=32, b=4),
          I["wdown"].transpose([1, 0, 2, 3]), s("wld"))
    v_wq = s("wld").n
    ptf_free = [[], []]
    for t8 in range(8):
        p.wait("sync", s("ta"), s("ta").n)
        p.dma("sync", x1t[:, :], x1i[t8 * 128:(t8 + 1) * 128, :], s("rl"))
        v_x = s("rl").n
        p.wait("tensor", s("rl"), v_x)
        for dt in range(32):
            pr = dt % 2
            wmulti("tensor", ptf_free[pr])
            p.run("tensor", lambda e, d=dt, pp=pr: e.transpose(
                PTF[pp][:, :], x1t[:, d * 128:(d + 1) * 128], c_identf[:, :]),
                s("ta"))
            p.wait("vector", s("ta"), s("ta").n)
            p.run("vector", lambda e, pp=pr, o=x1fm
                  [:, dt * 1024 + t8 * 128: dt * 1024 + (t8 + 1) * 128]:
                  e.tensor_copy(o, PTF[pp][:, :]), s("ev2"))
            ptf_free[pr] = [("ev2", s("ev2").n)]
    v_x1fm = s("ev2").n
    wb_free = []
    pob_free = [[], []]
    bout_free = [[], []]
    for oc in range(8):
        wmulti("gpsimd", wb_free)
        p.dma("gpsimd", wb2[:, :].rearrange("p (a j) -> p a j", a=32),
              I["wbase"][oc].transpose([1, 0, 2]), s("wld"))
        v_w = s("wld").n
        p.wait("tensor", s("wld"), v_w)
        p.wait("tensor", s("ev2"), v_x1fm)
        for t8 in range(8):
            pr = t8 % 2
            wmulti("tensor", pob_free[pr])
            for dt in range(32):
                p.run("tensor", lambda e, d=dt, t=t8, po=POB[pr][:, :]: e.matmul(
                    po, x1fm[:, d * 1024 + t * 128: d * 1024 + (t + 1) * 128],
                    wb2[:, d * 512:(d + 1) * 512],
                    start=(d == 0), stop=(d == 31)),
                    s("mm") if dt == 31 else None, 1)
            v_mm = s("mm").n
            p.wait("vector", s("mm"), v_mm)
            wmulti("vector", bout_free[pr])
            p.run("vector", lambda e, po=POB[pr][:, :], o=bout[pr][:, :]:
                  e.tensor_copy(o, po), s("ev2"))
            pob_free[pr] = [("ev2", s("ev2").n)]
            p.wait("sync", s("ev2"), s("ev2").n)
            p.dma("sync", O["basew"][t8 * 128:(t8 + 1) * 128,
                                     oc * 512:(oc + 1) * 512],
                  bout[pr][:, :], s("so"))
            bout_free[pr] = [("so", s("so").n)]
        wb_free = [("mm", s("mm").n)]
    p.wait("tensor", s("wld"), v_wq)
    for t8 in range(8):
        p.wait("tensor", s("ev2"), s("ev2").n if t8 == 0 else 0)
        for dt in range(32):
            p.run("tensor", lambda e, d=dt, t=t8: e.matmul(
                PQX[:, :], x1fm[:, d * 1024 + t * 128: d * 1024 + (t + 1) * 128],
                wq_sb[:, d * 128:(d + 1) * 128],
                start=(d == 0), stop=(d == 31)), s("mm") if dt == 31 else None, 1)
        v_mm = s("mm").n
        p.wait("vector", s("mm"), v_mm)
        p.wait("vector", s("gq"), s("gq").n)
        p.run("vector", lambda e: e.tensor_copy(qxt[:, :], PQX[:, :]), s("ev2"))
        v_q = s("ev2").n
        p.wait("tensor", s("ev2"), v_q)
        p.run("tensor", lambda e, st=(t8 == 0): e.matmul(
            PQ[:, :], c_onesbc[:, :], qxt[:, :], start=st, stop=(t8 == 7)),
            s("gq"))
    p.wait("vector", s("gq"), s("gq").n)
    p.run("vector", lambda e: e.tensor_copy(qrow[:, :], PQ[:, :]), s("ev2"))
    p.wait("sync", s("ev2"), s("ev2").n)
    p.dma("sync", O["qp"][:, :], qrow[:, :], s("so"))
    ph_free = [[], []]
    hout_free = [[], []]
    for rt in range(4):
        for th2 in range(2):
            pr = th2
            wmulti("tensor", ph_free[pr])
            for dt in range(32):
                p.run("tensor", lambda e, d=dt, r=rt, t=th2, po=PH2_[pr][:, :]:
                      e.matmul(po,
                               wd_sb[:, d * 512 + r * 128: d * 512 + (r + 1) * 128],
                               x1fm[:, d * 1024 + t * 512: d * 1024 + (t + 1) * 512],
                               start=(d == 0), stop=(d == 31)),
                      s("mm") if dt == 31 else None, 1)
            v_mm = s("mm").n
            p.wait("vector", s("mm"), v_mm)
            wmulti("vector", hout_free[pr])
            p.run("vector", lambda e, po=PH2_[pr][:, :], o=hout[pr][:, :]:
                  e.tensor_copy(o, po), s("ev2"))
            ph_free[pr] = [("ev2", s("ev2").n)]
            p.wait("sync", s("ev2"), s("ev2").n)
            p.dma("sync", O["hmat"][rt * 128:(rt + 1) * 128,
                                    th2 * 512:(th2 + 1) * 512],
                  hout[pr][:, :], s("so"))
            hout_free[pr] = [("so", s("so").n)]
    for nm in ("so", "st", "rl", "cld", "wld"):
        p.wait("sync", s(nm), s(nm).n)
    p.finalize()
    es3p.close()
    es3.close()
    es.close()
    return nc


def build_l3(debug=False):
    nc = bass.Bass("TRN2", target_bir_lowering=False, debug=False)
    es = ExitStack()
    p = Prog(nc, es)
    I = {}
    for nm, sh, dt in [
        ("qpall", [1, 1024], F32), ("keysrow", [1, 1024], F32),
        ("x1", [T, DM], F32), ("basew", [T, DM], F32),
        ("hmat", [512, T], BF16), ("wup", [4, 128, 8, 512], F32),
        ("onesrow", [1, 128], F32),
    ]:
        I[nm] = nc.dram_tensor(nm, sh, dt, kind="ExternalInput").ap()
    O = {}
    outs = [("out", [T, DM], F32)]
    if debug:
        outs += [("dbg_coef", [1, 8], F32), ("dbg_q", [1, 128], F32),
                 ("dbg_sc", [1, 8], F32), ("dbg_ex", [1, 8], F32)]
    for nm, sh, dt in outs:
        O[nm] = nc.dram_tensor(nm, sh, dt, kind="ExternalOutput").ap()
    sb = lambda nm, sh, dt: es.enter_context(nc.sbuf_tensor(nm, sh, dt))
    S = {nm: p.sem(nm) for nm in "cld wld rl mm ta ev2 ax so gq".split()}
    s = lambda nm: S[nm]

    def wmulti(eng, lst):
        for (nm, v) in lst:
            p.wait(eng, s(nm), v)

    c_ones1 = sb("c_ones1", [1, 128], F32)
    c_one11 = sb("c_one11", [1, 1], F32)
    c_eps = sb("c_eps", [128, 1], F32)
    keysb = sb("keysb", [1, 1024], F32)
    kq = sb("kq", [1, 1024], F32)
    qsb = sb("qsb", [1, 1024], F32)
    qrow = sb("qrow", [1, 128], F32)
    qcolb = sb("qcolb", [128, 1], F32)
    scrow = sb("scrow", [1, 8], F32)
    nmax = sb("nmax", [1, 1], F32)
    scdbg = sb("scdbg", [1, 8], F32)
    scdbg2 = sb("scdbg2", [1, 8], F32)
    ssum = sb("ssum", [1, 1], F32)
    srec = sb("srec", [1, 1], F32)
    ccols = sb("ccols", [128, 8], F32)
    hsb = sb("hsb", [128, 4 * T], BF16)
    hsc = sb("hsc", [128, 4 * T], BF16)
    wupb = sb("wupb", [128, 4 * 4096], BF16)
    x1sb = sb("x1sb", [128, DM], F32)
    bsb = sb("bsb", [128, DM], F32)
    x3 = sb("x3", [128, DM], F32)
    lnt = sb("lnt", [128, DM], F32)
    sqd = sb("sqd", [128, DM], F32)
    mcol = sb("mcol", [128, 1], F32)
    mcol2 = sb("mcol2", [128, 1], F32)
    vcol = sb("vcol", [128, 1], F32)
    otb = [sb(f"otb{i}", [128, DM], F32) for i in range(2)]

    p.dma("sync", c_ones1[:, :], I["onesrow"][:, :], s("cld"))
    p.dma("sync", qsb[:, :], I["qpall"][:, :], s("cld"))
    p.dma("sync", keysb[:, :], I["keysrow"][:, :], s("cld"))
    p.dma("gpsimd", hsb[:, :].rearrange("p (a t) -> p a t", a=4),
          I["hmat"].rearrange("(a p) t -> a p t", p=128).transpose([1, 0, 2]),
          s("wld"))
    p.dma("gpsimd", wupb[:, :].rearrange("p (a b j) -> p a b j", a=4, b=8),
          I["wup"].transpose([1, 0, 2, 3]), s("wld"))
    v_ld = s("cld").n
    v_wld = s("wld").n
    p.run("vector", lambda e: e.memset(c_one11[:, :], 1.0))
    p.run("vector", lambda e: e.memset(c_eps[:, :], EPS))
    p.wait("vector", s("cld"), v_ld)
    p.run("vector", lambda e: e.tensor_add(qrow[:, :], qsb[0:1, 0:128],
                                           qsb[0:1, 128:256]))
    for a in range(2, 8):
        p.run("vector", lambda e, o=a * 128: e.tensor_add(
            qrow[:, :], qrow[:, :], qsb[0:1, o:o + 128]))
    p.run("vector", lambda e: e.nop(), s("ev2"))
    with ExitStack() as esQ:
        PCB = esQ.enter_context(nc.psum_tensor("PCB", [128, 8], F32))
        p.wait("vector", s("ev2"), s("ev2").n)
        for e_ in range(8):
            p.run("vector", lambda e, o=kq[0:1, e_ * 128:(e_ + 1) * 128],
                  k_=keysb[0:1, e_ * 128:(e_ + 1) * 128]:
                  e.tensor_mul(o, k_, qrow[:, :]))
            p.run("vector", lambda e, o=scrow[0:1, e_:e_ + 1],
                  i_=kq[0:1, e_ * 128:(e_ + 1) * 128]:
                  e.reduce_sum(o, i_, axis=AX.X),
                  s("ev2") if e_ == 7 else None, 1)
        p.wait("scalar", s("ev2"), s("ev2").n)
        p.run("scalar", lambda e: e.activation(scrow[:, :], scrow[:, :],
                                               AF.Identity,
                                               scale=float(1.0 / (8192.0 * np.sqrt(128)))),
              s("ax"))
        if debug:
            p.run("scalar", lambda e: e.copy(scdbg[:, :], scrow[:, :]), s("ax"))
            p.wait("sync", s("ax"), s("ax").n)
            p.dma("sync", O["dbg_q"][:, :], qrow[:, :], s("so"))
            p.dma("sync", O["dbg_sc"][:, :], scdbg[:, :], s("so"))
        p.wait("vector", s("ax"), s("ax").n)
        p.run("vector", lambda e: e.reduce_max(nmax[:, :], scrow[:, :],
                                               axis=AX.X, negate=True),
              s("ev2"))
        p.wait("scalar", s("ev2"), s("ev2").n)
        p.run("scalar", lambda e: e.activation(scrow[:, :], scrow[:, :], AF.Exp,
                                               bias=nmax[:, :]), s("ax"))
        if debug:
            p.run("scalar", lambda e: e.copy(scdbg2[:, :], scrow[:, :]), s("ax"))
        p.wait("vector", s("ax"), s("ax").n)
        p.run("vector", lambda e: e.reduce_sum(ssum[:, :], scrow[:, :],
                                               axis=AX.X), s("ev2"))
        p.wait("scalar", s("ev2"), s("ev2").n)
        p.run("scalar", lambda e: e.activation(srec[:, :], ssum[:, :], AF.Ln))
        p.run("scalar", lambda e: e.activation(srec[:, :], srec[:, :], AF.Exp,
                                               scale=-1.0))
        p.run("scalar", lambda e: e.activation(scrow[:, :], scrow[:, :],
                                               AF.Identity,
                                               scale=srec[0:1, :]), s("ax"))
        v_cf = s("ax").n
        if debug:
            p.wait("sync", s("ax"), v_cf)
            p.dma("sync", O["dbg_coef"][:, :], scrow[:, :], s("so"))
            p.dma("sync", O["dbg_ex"][:, :], scdbg2[:, :], s("so"))
        p.wait("tensor", s("ax"), v_cf)
        p.run("tensor", lambda e: e.matmul(PCB[:, :], c_ones1[:, :],
                                           scrow[:, :], start=True, stop=True),
              s("ta"))
        p.wait("vector", s("ta"), s("ta").n)
        p.run("vector", lambda e: e.tensor_copy(ccols[:, :], PCB[:, :]),
              s("ev2"))
    v_cc = s("ev2").n
    p.wait("vector", s("wld"), v_wld)
    for rt in range(4):
        for bh in range(2):
            e_ = 2 * rt + bh
            b = bh * 64
            p.run("vector", lambda e, o=hsc[b:b + 64, rt * T:(rt + 1) * T],
                  i_=hsb[b:b + 64, rt * T:(rt + 1) * T],
                  sc_=ccols[b:b + 64, e_:e_ + 1]:
                  e.tensor_scalar_mul(o, i_, sc_), s("ev2"))
    v_hs = s("ev2").n
    with ExitStack() as esU:
        PUP = [esU.enter_context(nc.psum_tensor(f"PUP{i}", [128, 512], F32))
               for i in range(2)]
        pup_free = [[], []]
        otb_free = [[], []]
        for t8 in range(8):
            p.wait("sync", s("ax"), s("ax").n)
            p.dma("sync", x1sb[:, :], I["x1"][t8 * 128:(t8 + 1) * 128, :],
                  s("rl"))
            p.dma("sync", bsb[:, :], I["basew"][t8 * 128:(t8 + 1) * 128, :],
                  s("rl"))
            v_xb = s("rl").n
            p.wait("tensor", s("ev2"), v_hs)
            for oc in range(8):
                pr = oc % 2
                wmulti("tensor", pup_free[pr])
                for rt in range(4):
                    p.run("tensor", lambda e, r=rt, o=oc, po=PUP[pr][:, :],
                          l=hsc[:, rt * T + t8 * 128: rt * T + (t8 + 1) * 128]:
                          e.matmul(po, l,
                                   wupb[:, r * 4096 + o * 512:
                                        r * 4096 + (o + 1) * 512],
                                   start=(r == 0), stop=(r == 3)),
                          s("mm") if rt == 3 else None, 1)
                v_mm = s("mm").n
                p.wait("vector", s("mm"), v_mm)
                p.wait("vector", s("rl"), v_xb)
                p.run("vector", lambda e, po=PUP[pr][:, :],
                      o=lnt[:, oc * 512:(oc + 1) * 512],
                      b_=bsb[:, oc * 512:(oc + 1) * 512]:
                      e.tensor_add(o, po, b_))
                p.run("vector", lambda e,
                      o=x3[:, oc * 512:(oc + 1) * 512],
                      a=lnt[:, oc * 512:(oc + 1) * 512],
                      b_=x1sb[:, oc * 512:(oc + 1) * 512]:
                      e.tensor_add(o, a, b_), s("ev2"))
                pup_free[pr] = [("ev2", s("ev2").n)]
            p.run("vector", lambda e: e.tensor_copy(lnt[:, :], x3[:, :]))
            p.run("vector", lambda e: e.reduce_sum(mcol[:, :], lnt[:, :],
                                                   axis=AX.X))
            p.run("vector", lambda e: e.memset(vcol[:, :], 0.0), s("ev2"))
            p.wait("scalar", s("ev2"), s("ev2").n)
            p.run("scalar", lambda e: e.activation(mcol2[:, :], mcol[:, :],
                                                   AF.Identity, scale=1.0 / DM),
                  s("ax"))
            p.wait("vector", s("ax"), s("ax").n)
            par = t8 % 2
            wmulti("vector", otb_free[par])
            p.run("vector", lambda e, o=otb[par][:, :]: e.tensor_scalar(
                o, lnt[:, :], mcol2[:, :], None, ALU.subtract), s("ev2"))
            p.wait("scalar", s("ev2"), s("ev2").n)
            p.run("scalar", lambda e, a=otb[par][:, :]: e.activation(
                sqd[:, :], a, AF.Square, accum_out=vcol[:, :]))
            p.run("scalar", lambda e: e.activation(vcol[:, :], vcol[:, :],
                                                   AF.Ln, scale=1.0 / DM,
                                                   bias=c_eps[:, :]))
            p.run("scalar", lambda e: e.activation(vcol[:, :], vcol[:, :],
                                                   AF.Exp, scale=-0.5), s("ax"))
            p.wait("vector", s("ax"), s("ax").n)
            p.run("vector", lambda e, a=otb[par][:, :]: e.tensor_scalar_mul(
                a, a, vcol[:, :]), s("ev2"))
            v_o = s("ev2").n
            p.wait("sync", s("ev2"), v_o)
            p.dma("sync", O["out"][t8 * 128:(t8 + 1) * 128, :], otb[par][:, :],
                  s("so"))
            otb_free[par] = [("so", s("so").n)]
            p.run("sync", lambda e: e.nop(), s("ax"), 1)
    for nm in ("so", "rl", "cld", "wld"):
        p.wait("sync", s(nm), s(nm).n)
    p.finalize()
    es.close()
    return nc


def kernel(**inputs):
    x = inputs["L0_attn_input_states"][0].astype(np.float32, copy=False)
    shared1, per1 = stage_inputs(inputs)
    nc1 = build_l1(debug=False)
    im1 = []
    for k in range(8):
        d = dict(shared1); d.update(per1[k])
        im1.append(d)
    r1 = run_bass_kernel_spmd(nc1, im1, core_ids=list(range(8))).results

    relay = stage_l2_relay(r1)
    w2 = stage_l2(inputs)
    ident = np.eye(128, dtype=np.float32)
    tri7 = np.where(np.arange(7)[:, None] < np.arange(7)[None, :], 1, 0).astype(np.float32)
    shared2 = dict(w2)
    shared2.update(identf=ident, identb=ident.astype(ml_dtypes.bfloat16),
                   onesrow=np.ones((1, 128), np.float32), tri7=tri7)
    nc2 = build_l2(debug=False)
    im2 = []
    for k in range(8):
        d = dict(shared2); d.update(relay[k])
        d["x0tok"] = np.ascontiguousarray(x[1024 * k:1024 * (k + 1)])
        im2.append(d)
    r2 = run_bass_kernel_spmd(nc2, im2, core_ids=list(range(8))).results

    w3 = stage_l3(inputs)
    qpall = np.ascontiguousarray(
        np.concatenate([r2[k]["qp"].reshape(128) for k in range(8)])
        .reshape(1, 1024).astype(np.float32))
    nc3 = build_l3(debug=False)
    im3 = []
    for k in range(8):
        im3.append(dict(qpall=qpall, keysrow=w3["keysrow"], wup=w3["wup"],
                        x1=r2[k]["x1"], basew=r2[k]["basew"],
                        hmat=r2[k]["hmat"],
                        onesrow=np.ones((1, 128), np.float32)))
    r3 = run_bass_kernel_spmd(nc3, im3, core_ids=list(range(8))).results
    out = np.concatenate([r3[k]["out"] for k in range(8)], axis=0)
    return out.reshape(1, 8192, 4096).astype(np.float32, copy=False)


# version 02
# speedup vs baseline: 1.0009x; 1.0009x over previous
import sys
sys.path.insert(0, '/opt/trn_rl_repo')
from contextlib import ExitStack
import numpy as np
import ml_dtypes
import concourse.bass as bass
import concourse.mybir as mybir
from concourse.bass_utils import run_bass_kernel_spmd

F32 = mybir.dt.float32
BF16 = mybir.dt.bfloat16
AF = mybir.ActivationFunctionType
ALU = mybir.AluOpType
AX = mybir.AxisListType

T, HALO, CS, NCH = 1024, 3, 256, 4
DM, GATE, HS, NB, NCC, DT = 4096, 8192, 8192, 1024, 1024, 128
XBC = HS + NB + NCC
PROJ = GATE + XBC + DT
H, P, N, G = 128, 64, 128, 8
NCT_XBC, NCT_GATE = XBC // 128, GATE // 128
NCT = PROJ // 128
TH = T + HALO
NEG = -1e30
EPS = 1e-5

ENGS = ("sync", "tensor", "vector", "scalar", "gpsimd")


class Sem:
    def __init__(self, es, nc, name):
        self.h = es.enter_context(nc.semaphore(name))
        self.n = 0


class Prog:
    def __init__(self, nc, es):
        self.nc, self.es = nc, es
        self.q = {e: [] for e in ENGS}

    def sem(self, name):
        return Sem(self.es, self.nc, name)

    def wait(self, eng, sem, val=None):
        v = sem.n if val is None else val
        self.q[eng].append(lambda e, s=sem.h, v=v: e.wait_ge(s, v))

    def run(self, eng, fn, inc=None, k=1):
        if inc is not None:
            inc.n += k
            self.q[eng].append(lambda e, f=fn, s=inc.h, kk=k: f(e).then_inc(s, kk))
        else:
            self.q[eng].append(fn)

    def dma(self, eng, out, in_, inc):
        self.run(eng, lambda e, o=out, i=in_: e.dma_start(out=o, in_=i), inc, 16)

    def finalize(self):
        with self.nc.Block() as block:
            @block.sync
            def _(e):
                for f in self.q["sync"]: f(e)
            @block.tensor
            def _(e):
                for f in self.q["tensor"]: f(e)
            @block.vector
            def _(e):
                for f in self.q["vector"]: f(e)
            @block.scalar
            def _(e):
                for f in self.q["scalar"]: f(e)
            @block.gpsimd
            def _(e):
                for f in self.q["gpsimd"]: f(e)


def stage_inputs(inputs):
    x = np.ascontiguousarray(inputs["L0_attn_input_states"][0])
    W = inputs["L0_attn_in_proj_w"]
    Wre = np.concatenate([W[GATE:GATE + XBC], W[GATE + XBC:], W[:GATE]], axis=0)
    wip = np.ascontiguousarray(
        Wre.reshape(NCT, 128, 32, 128).transpose(0, 3, 2, 1))
    convw = np.ascontiguousarray(
        inputs["L0_attn_conv_w"][:, 0, :].reshape(80, 128, 4).transpose(1, 0, 2)
        .reshape(128, 320))
    convb = np.ascontiguousarray(
        inputs["L0_attn_conv_b"].reshape(80, 128).T)
    ident = np.eye(128, dtype=np.float32)
    maskd = np.where(np.arange(128)[None, :] >= np.arange(128)[:, None],
                     np.float32(0), np.float32(NEG))
    maskd4 = np.ascontiguousarray(np.tile(maskd, (1, 4)))
    shared = dict(
        wip=wip,
        convw=convw, convb=convb,
        dtb=np.ascontiguousarray(inputs["L0_attn_dt_bias"].reshape(128, 1)),
        dtbrow=np.ascontiguousarray(inputs["L0_attn_dt_bias"].reshape(1, 128)),
        alogcol=np.ascontiguousarray(inputs["L0_attn_A_log"].reshape(128, 1)),
        drow=np.ascontiguousarray(inputs["L0_attn_D"].reshape(1, 128)),
        identf=ident,
        identb=ident.astype(ml_dtypes.bfloat16),
        maskd4=maskd4,
        onesrow=np.ones((1, 128), np.float32),
    )
    per_core = []
    for k in range(8):
        sl = x[1024 * k:1024 * (k + 1)]
        xh = np.zeros((TH, DM), np.float32)
        lo = 1024 * k - HALO
        s0 = max(lo, 0)
        xh[s0 - lo:] = x[s0:1024 * (k + 1)]
        xfm = np.ascontiguousarray(xh.reshape(TH, 32, 128).transpose(2, 1, 0))
        per_core.append(dict(xfm=xfm))
    return shared, per_core


def stage_l2(inputs):
    outw = inputs["L0_attn_out_w"]
    wop = np.ascontiguousarray(
        outw.reshape(8, 512, 64, 128).transpose(0, 2, 3, 1))
    worg = inputs["L0_ffn_W_org"]
    wbase = np.ascontiguousarray(
        worg.reshape(8, 512, 32, 128).transpose(0, 2, 3, 1))
    down = inputs["L0_ffn_lora_down"]
    dcat = np.concatenate([down[e] for e in range(8)], axis=1)
    wdown = np.ascontiguousarray(
        dcat.reshape(32, 128, 4, 128).transpose(0, 1, 2, 3))
    wq = inputs["L0_ffn_Wq"]
    wqT = np.ascontiguousarray(wq.T.reshape(32, 128, 128))
    normw = np.ascontiguousarray(
        inputs["L0_attn_norm_w"].reshape(64, 128).T)
    return dict(wop=wop, wbase=wbase, wdown=wdown, wqT=wqT, normw=normw)


def stage_l3(inputs):
    up = inputs["L0_ffn_lora_up"]
    ucat = np.concatenate([up[e] for e in range(8)], axis=0)
    wup = np.ascontiguousarray(
        ucat.reshape(4, 128, 8, 512))
    keysrow = np.ascontiguousarray(
        inputs["L0_ffn_lora_keys"].reshape(1, 1024))
    return dict(wup=wup, keysrow=keysrow)


def build_l1(debug=False):
    nc = bass.Bass("TRN2", target_bir_lowering=False, debug=False)
    es = ExitStack()
    p = Prog(nc, es)

    I = {}
    for nm, sh, dt in [
        ("xfm", [128, 32, TH], F32), ("wip", [NCT, 128, 32, 128], F32),
        ("convw", [128, 320], F32), ("convb", [128, 80], F32),
        ("dtb", [128, 1], F32), ("dtbrow", [1, 128], F32),
        ("alogcol", [128, 1], F32), ("drow", [1, 128], F32),
        ("identf", [128, 128], F32), ("identb", [128, 128], BF16),
        ("maskd4", [128, 512], F32), ("onesrow", [1, 128], F32),
    ]:
        I[nm] = nc.dram_tensor(nm, sh, dt, kind="ExternalInput").ap()
    O = {}
    outs = [("yp", [HS, T], BF16), ("sg", [GATE, T], BF16),
            ("sst", [NCH, 128, 8192], F32), ("score", [128, 8192], F32),
            ("cum", [128, T], F32), ("clast", [NCH, 128], F32),
            ("cfm", [8, 128, T], BF16)]
    if debug:
        outs += [("dbg_xbcs", [80, 128, T], BF16), ("dbg_dtsp", [128, T], F32),
                 ("dbg_biastok", [128, T], F32), ("dbg_msc", [128, 128, 384], BF16),
                 ("dbg_sexp", [NCH, 128, 256], F32)]
    for nm, sh, dt in outs:
        O[nm] = nc.dram_tensor(nm, sh, dt, kind="ExternalOutput").ap()
    xbcst = nc.dram_tensor("xbcst", [80, 128, T], BF16, kind="Internal").ap()
    cumrows = nc.dram_tensor("cumrows", [NCH, 2, 128 * 128], F32, kind="Internal").ap()

    sb = lambda nm, sh, dt: es.enter_context(nc.sbuf_tensor(nm, sh, dt))
    S = {nm: p.sem(nm) for nm in
         ("cld xld wld mm evA cv sl stX stO so ta ev2 ax rl dq sc ms ym ye "
          "sm se pw").split()}
    s = lambda nm: S[nm]

    c_identf = sb("c_identf", [128, 128], F32)
    c_identb = sb("c_identb", [128, 128], BF16)
    c_maskd4 = sb("c_maskd4", [128, 512], F32)
    c_convw = sb("c_convw", [128, 320], F32)
    c_convb = sb("c_convb", [128, 80], F32)
    c_dtb = sb("c_dtb", [128, 1], F32)
    c_alog = sb("c_alog", [128, 1], F32)
    c_ones1 = sb("c_ones1", [1, 128], F32)
    c_onesT = sb("c_onesT", [128, 256], F32)
    r_dtbrow = sb("r_dtbrow", [1, 128], F32)
    r_drow = sb("r_drow", [1, 128], F32)
    dtbB = sb("dtbB", [128, 128], F32)
    Db = sb("Db", [128, 128], F32)
    cum_fm = sb("cum_fm", [128, T], F32)
    bias_tok = sb("bias_tok", [128, T], F32)
    xbc_ch = sb("xbc_ch", [128, 80 * 256], BF16)

    for nm, dst in [("identf", c_identf), ("identb", c_identb),
                    ("maskd4", c_maskd4), ("convw", c_convw),
                    ("convb", c_convb), ("dtb", c_dtb), ("alogcol", c_alog),
                    ("dtbrow", r_dtbrow), ("drow", r_drow)]:
        p.dma("sync", dst[:, :], I[nm][:, :], s("cld"))
    v_cld = s("cld").n
    p.run("vector", lambda e: e.memset(c_ones1[:, :], 1.0))
    p.run("vector", lambda e: e.memset(c_onesT[:, :], 1.0), s("ev2"))
    v_ones = s("ev2").n

    es1 = ExitStack()
    sb1 = lambda nm, sh, dt: es1.enter_context(nc.sbuf_tensor(nm, sh, dt))
    x_fm = sb1("x_fm", [128, 32 * TH], BF16)
    Wb = [sb1(f"Wb{i}", [128, 4096], BF16) for i in range(2)]
    raw = [sb1(f"raw{i}", [128, TH], F32) for i in range(2)]
    caccA = sb1("caccA", [128, T], F32)
    caccB = sb1("caccB", [128, T], F32)
    outt = [sb1(f"outt{i}", [128, T], BF16) for i in range(2)]
    dt_fm = sb1("dt_fm", [128, T], F32)
    t1f = sb1("t1f", [128, T], F32)
    t2f = sb1("t2f", [128, T], F32)
    cum_tok = sb1("cum_tok", [128, T], F32)

    es1p = ExitStack()
    ps1 = lambda nm, sh: es1p.enter_context(nc.psum_tensor(nm, sh, F32))
    PA0 = [ps1(f"PA0{i}", [128, 512]) for i in range(2)]
    PA1 = [ps1(f"PA1{i}", [128, 512]) for i in range(2)]
    PAt = [ps1(f"PAt{i}", [128, 512]) for i in range(2)]
    PTP = [ps1(f"PTP{i}", [128, 128]) for i in range(2)]

    p.dma("gpsimd", x_fm[:, :], I["xfm"].rearrange("p a t -> p (a t)"), s("xld"))
    p.wait("tensor", s("cld"), v_cld)
    p.wait("tensor", s("ev2"), v_ones)
    p.run("tensor", lambda e: e.matmul(PTP[0][:, :], c_ones1[:, :],
                                       r_dtbrow[:, :], start=True, stop=True))
    p.run("tensor", lambda e: e.matmul(PTP[1][:, :], c_ones1[:, :],
                                       r_drow[:, :], start=True, stop=True), s("ta"))
    p.wait("vector", s("ta"))
    p.run("vector", lambda e: e.tensor_copy(dtbB[:, :], PTP[0][:, :]))
    p.run("vector", lambda e: e.tensor_copy(Db[:, :], PTP[1][:, :]), s("ev2"))
    v_bc0 = s("ev2").n
    wb_free = [[], []]
    pa_free = [[], []]
    raw_free = [[], []]
    cacc_free = [[], []]
    outt_free = [[], []]
    ptp_free = [[("ev2", v_bc0)], [("ev2", v_bc0)]]

    def wmulti(eng, lst):
        for (nm, v) in lst:
            p.wait(eng, s(nm), v)

    def ctile(ct, kind):
        par = ct % 2
        wmulti("gpsimd", wb_free[par])
        p.dma("gpsimd", Wb[par][:, :],
              I["wip"][ct].rearrange("p a c -> p (a c)"), s("wld"))
        p.wait("tensor", s("wld"), 16 * s("wld").n // 16)
        if ct == 0:
            p.wait("tensor", s("xld"), 16)
        wmulti("tensor", pa_free[par])
        slices = ([(0, 512, PA0[par]), (512, 1024, PA1[par]), (1024, TH, PAt[par])]
                  if kind == "xbc" else [(3, 515, PA0[par]), (515, TH, PA1[par])])
        nsl = len(slices)
        for dti in range(32):
            lhs = Wb[par][:, dti * 128:(dti + 1) * 128]
            for si, (c0, c1, ps_) in enumerate(slices):
                w = c1 - c0
                rhs = x_fm[:, dti * TH + c0: dti * TH + c1]
                p.run("tensor",
                      lambda e, o=ps_[:, 0:w], l=lhs, r=rhs, st=(dti == 0),
                      sp=(dti == 31): e.matmul(o, l, r, start=st, stop=sp),
                      s("mm") if (dti == 31 and si == nsl - 1) else None, 1)
        v_mm = s("mm").n
        wb_free[par] = [("mm", v_mm)]
        if kind == "xbc":
            wmulti("vector", raw_free[par])
            p.wait("vector", s("mm"), v_mm)
            p.run("vector", lambda e, pr=par: e.tensor_copy(raw[pr][:, 0:512],
                                                            PA0[pr][:, :]))
            p.run("vector", lambda e, pr=par: e.tensor_copy(raw[pr][:, 512:1024],
                                                            PA1[pr][:, :]))
            p.run("vector", lambda e, pr=par: e.tensor_copy(raw[pr][:, 1024:TH],
                                                            PAt[pr][:, 0:3]),
                  s("evA"))
            v_ev = s("evA").n
            pa_free[par] = [("evA", v_ev)]
            wmulti("vector", cacc_free[0])
            if ct == 0:
                p.wait("vector", s("cld"), v_cld)
            wc = lambda k: c_convw[:, ct * 4 + k: ct * 4 + k + 1]
            p.run("vector", lambda e, pr=par, w0=wc(0): e.tensor_scalar(
                caccA[:, :], raw[pr][:, 0:T], w0, None, ALU.mult))
            p.run("vector", lambda e, pr=par, w1=wc(1): e.scalar_tensor_tensor(
                caccB[:, :], raw[pr][:, 1:1 + T], w1, caccA[:, :],
                ALU.mult, ALU.add))
            p.run("vector", lambda e, pr=par, w2=wc(2): e.scalar_tensor_tensor(
                caccA[:, :], raw[pr][:, 2:2 + T], w2, caccB[:, :],
                ALU.mult, ALU.add))
            p.run("vector", lambda e, pr=par, w3=wc(3): e.scalar_tensor_tensor(
                caccB[:, :], raw[pr][:, 3:3 + T], w3, caccA[:, :],
                ALU.mult, ALU.add), s("cv"))
            v_cv = s("cv").n
            raw_free[par] = []
            wmulti("scalar", outt_free[par])
            p.wait("scalar", s("cv"), v_cv)
            p.run("scalar", lambda e, pr=par, b=c_convb[:, ct:ct + 1]:
                  e.activation(outt[pr][:, :], caccB[:, :], AF.Silu, bias=b),
                  s("sl"))
            v_sl = s("sl").n
            cacc_free[0] = [("sl", v_sl)]
            p.wait("sync", s("sl"), v_sl)
            p.dma("sync", xbcst[ct], outt[par][:, :], s("stX"))
            outt_free[par] = [("stX", s("stX").n)]
            if 72 <= ct < 80:
                p.dma("sync", O["cfm"][ct - 72], outt[par][:, :], s("stO"))
                outt_free[par].append(("stO", s("stO").n))
            if debug:
                p.dma("sync", O["dbg_xbcs"][ct], outt[par][:, :], s("stO"))
                outt_free[par].append(("stO", s("stO").n))
        elif kind == "gate":
            gi = ct - 81
            wmulti("scalar", outt_free[par])
            p.wait("scalar", s("mm"), v_mm)
            p.run("scalar", lambda e, pr=par: e.activation(
                outt[pr][:, 0:512], PA0[pr][:, :], AF.Silu))
            p.run("scalar", lambda e, pr=par: e.activation(
                outt[pr][:, 512:1024], PA1[pr][:, :], AF.Silu), s("evA"))
            v_ev = s("evA").n
            pa_free[par] = [("evA", v_ev)]
            p.wait("sync", s("evA"), v_ev)
            p.dma("sync", O["sg"][gi * 128:(gi + 1) * 128, :], outt[par][:, :],
                  s("stO"))
            outt_free[par] = [("stO", s("stO").n)]
        else:
            p.wait("vector", s("mm"), v_mm)
            p.run("vector", lambda e, pr=par: e.tensor_copy(dt_fm[:, 0:512],
                                                            PA0[pr][:, :]))
            p.run("vector", lambda e, pr=par: e.tensor_copy(dt_fm[:, 512:1024],
                                                            PA1[pr][:, :]),
                  s("evA"))
            pa_free[par] = [("evA", s("evA").n)]

    for ct in range(80):
        ctile(ct, "xbc")
    ctile(80, "dt")

    v_dtfm = s("evA").n
    p.wait("scalar", s("evA"), v_dtfm)
    p.run("scalar", lambda e: e.activation(t1f[:, :], dt_fm[:, :], AF.Relu,
                                           bias=c_dtb[:, :]))
    p.run("scalar", lambda e: e.activation(t2f[:, :], dt_fm[:, :], AF.Abs,
                                           bias=c_dtb[:, :]))
    p.run("scalar", lambda e: e.activation(t2f[:, :], t2f[:, :], AF.Exp,
                                           scale=-1.0))
    p.run("scalar", lambda e: e.activation(t2f[:, :], t2f[:, :], AF.Ln, bias=1.0))
    p.run("scalar", lambda e: e.activation(c_alog[:, :], c_alog[:, :], AF.Exp),
          s("ax"))
    p.wait("vector", s("ax"))
    p.run("vector", lambda e: e.tensor_add(dt_fm[:, :], t1f[:, :], t2f[:, :]),
          s("ev2"))
    if debug:
        p.wait("sync", s("ev2"), s("ev2").n)
        p.dma("sync", O["dbg_dtsp"][:, :], dt_fm[:, :], s("stO"))
    p.run("vector", lambda e: e.tensor_scalar(t1f[:, :], dt_fm[:, :],
                                              c_alog[:, :], -1.0,
                                              ALU.mult, ALU.mult))
    for c in range(NCH):
        p.run("vector", lambda e, c0=c * CS: e.tensor_tensor_scan(
            cum_fm[:, c0:c0 + CS], c_onesT[:, :], t1f[:, c0:c0 + CS], 0.0,
            ALU.mult, ALU.add), s("ev2"))
    v_cum = s("ev2").n
    p.wait("sync", s("ev2"), v_cum)
    p.dma("sync", O["cum"][:, :], cum_fm[:, :], s("stO"))
    for c in range(NCH):
        p.dma("sync", cumrows[c].rearrange("b (h i) -> h b i", i=128),
              cum_fm[:, c * CS:(c + 1) * CS].rearrange("h (b i) -> h b i", b=2),
              s("rl"))
    v_cumrows = s("rl").n
    p.wait("tensor", s("ev2"), v_cum)
    for tt in range(8):
        par = tt % 2
        wmulti("tensor", ptp_free[par])
        p.run("tensor", lambda e, pr=par, t0=tt * 128: e.transpose(
            PTP[pr][:, :], cum_fm[:, t0:t0 + 128], c_identf[:, :]), s("ta"))
        p.wait("vector", s("ta"), s("ta").n)
        p.run("vector", lambda e, pr=par, t0=tt * 128: e.tensor_copy(
            cum_tok[:, t0:t0 + 128], PTP[pr][:, :]), s("ev2"))
        ptp_free[par] = [("ev2", s("ev2").n)]
    for tt in range(8):
        par = tt % 2
        wmulti("tensor", pa_free[par])
        p.wait("tensor", s("ev2"), v_bc0)
        for dti in range(32):
            p.run("tensor", lambda e, pr=par, d=dti, t0=3 + tt * 128:
                  e.matmul(PAt[pr][:, 0:128],
                           x_fm[:, d * TH + t0: d * TH + t0 + 128],
                           Wb[0][:, d * 128:(d + 1) * 128],
                           start=(d == 0), stop=(d == 31)),
                  s("ta") if dti == 31 else None, 1)
        v_ta = s("ta").n
        p.wait("vector", s("ta"), v_ta)
        p.run("vector", lambda e, pr=par, t0=tt * 128: e.tensor_add(
            t2f[:, t0:t0 + 128], PAt[pr][:, 0:128], dtbB[:, :]), s("ev2"))
        pa_free[par] = [("ev2", s("ev2").n)]
        p.wait("scalar", s("ev2"), s("ev2").n)
        p.run("scalar", lambda e, t0=tt * 128: e.activation(
            t1f[:, t0:t0 + 128], t2f[:, t0:t0 + 128], AF.Abs))
        p.run("scalar", lambda e, t0=tt * 128: e.activation(
            t1f[:, t0:t0 + 128], t1f[:, t0:t0 + 128], AF.Exp, scale=-1.0))
        p.run("scalar", lambda e, t0=tt * 128: e.activation(
            t1f[:, t0:t0 + 128], t1f[:, t0:t0 + 128], AF.Ln, bias=1.0), s("ax"))
        p.wait("vector", s("ax"), s("ax").n)
        p.run("vector", lambda e, t0=tt * 128: e.tensor_scalar_max(
            bias_tok[:, t0:t0 + 128], t2f[:, t0:t0 + 128], 0.0))
        p.run("vector", lambda e, t0=tt * 128: e.tensor_add(
            t2f[:, t0:t0 + 128], bias_tok[:, t0:t0 + 128], t1f[:, t0:t0 + 128]),
            s("ev2"))
    wb_free[0] = [("ta", s("ta").n)]
    p.wait("scalar", s("ev2"), s("ev2").n)
    p.run("scalar", lambda e: e.activation(t1f[:, :], t2f[:, :], AF.Ln), s("ax"))
    p.wait("vector", s("ax"), s("ax").n)
    p.run("vector", lambda e: e.tensor_sub(bias_tok[:, :], t1f[:, :],
                                           cum_tok[:, :]), s("ev2"))
    v_biastok = s("ev2").n
    if debug:
        p.wait("sync", s("ev2"), v_biastok)
        p.dma("sync", O["dbg_biastok"][:, :], bias_tok[:, :], s("stO"))

    for ct in range(81, 145):
        ctile(ct, "gate")
    v_gate_mm = s("mm").n
    v_gate_stO = s("stO").n
    v_gate_ta = s("ta").n
    es1p.close()
    es1.close()
    p.finalize_l1_tail = None
    return _l1_tail(nc, es, p, I, O, xbcst, cumrows, sb, S, debug,
                    dict(c_identf=c_identf, c_identb=c_identb,
                         c_maskd4=c_maskd4, c_ones1=c_ones1, Db=Db,
                         cum_fm=cum_fm, bias_tok=bias_tok, xbc_ch=xbc_ch),
                    v_gate_mm, v_gate_stO, v_biastok, v_cum, v_cumrows)


def _l1_tail(nc, es, p, I, O, xbcst, cumrows, sb, S, debug, R,
             v_gate_mm, v_gate_stO, v_biastok, v_cum, v_cumrows):
    s = lambda nm: S[nm]
    c_identf, c_identb, c_maskd4 = R["c_identf"], R["c_identb"], R["c_maskd4"]
    c_ones1, Db = R["c_ones1"], R["Db"]
    cum_fm, bias_tok = R["cum_fm"], R["bias_tok"]
    xbc_ch = R["xbc_ch"]
    hstok = sb("hstok", [128, 2 * 8192], BF16)
    btok = sb("btok", [128, 2 * 1024], BF16)

    prev = sb("prev", [128, 8192], F32)
    sstb = sb("sstb", [128, 8192], F32)
    pwtmp = sb("pwtmp", [128, 512], F32)
    crow = [sb(f"crow{i}", [1, 2048], F32) for i in range(2)]
    clrow = sb("clrow", [1, 128], F32)
    cdrow = sb("cdrow", [1, 128], F32)
    sinb = sb("sinb", [128, 256], F32)
    sexp = sb("sexp", [128, 256], F32)
    decay = [sb(f"decay{i}", [128, 384], F32) for i in range(2)]
    msc = [sb(f"msc{i}", [128, 384], BF16) for i in range(2)]
    hssb = sb("hssb", [128, 128], BF16)
    dhsb = sb("dhsb", [128, 128], BF16)
    ypsb = [sb(f"ypsb{i}", [64, 256], BF16) for i in range(2)]

    def barrier(eng):
        for nm in S:
            p.wait(eng, s(nm), s(nm).n)

    barrier("vector"); barrier("scalar"); barrier("sync")
    p.run("vector", lambda e: e.memset(prev[:, :], 0.0), s("pw"))

    ps_free = [[], []]; decay_free = [[], []]; msc_free = [[], []]
    py_free = [[], []]; ypsb_free = [[], []]
    hss_free = []; dhs_free = []
    pdq_free = []; pst_free = []; crow_free = []
    clrow_free = []; cdrow_free = []; sstb_free = []
    xbcch_free = []; hstok_free = []; btok_free = []
    ptb_guard = []
    sexp_free = []

    def wmulti(eng, lst):
        for (nm, v) in lst:
            p.wait(eng, s(nm), v)

    for ch in range(NCH):
        p.wait("sync", s("stX"), 16 * 80)
        wmulti("sync", xbcch_free)
        for d5 in range(16):
            p.dma("sync", xbc_ch[:, d5 * 5 * 256:(d5 + 1) * 5 * 256],
                  xbcst[d5 * 5:(d5 + 1) * 5, :, ch * CS:(ch + 1) * CS]
                  .transpose([1, 0, 2]), s("rl"))
        v_rlx = s("rl").n
        with ExitStack() as esA:
            PTB = [esA.enter_context(nc.psum_tensor(f"PTB{ch}_{i}", [128, 128],
                                                    BF16)) for i in range(2)]
            ptbf = [list(ptb_guard), list(ptb_guard)]
            p.wait("tensor", s("rl"), v_rlx)
            wmulti("vector", hstok_free); wmulti("vector", btok_free)
            ntp = 0
            for tt2 in range(2):
                for cht in range(72):
                    par = ntp % 2
                    wmulti("tensor", ptbf[par])
                    src = (xbc_ch[:, cht * 256 + tt2 * 128: cht * 256 + tt2 * 128 + 128]
                           if cht < 64 else
                           xbc_ch[:, (cht) * 256 + tt2 * 128: (cht) * 256 + tt2 * 128 + 128])
                    p.run("tensor", lambda e, pr=par, sr=src: e.transpose(
                        PTB[pr][:, :], sr, c_identb[:, :]), s("ta"))
                    p.wait("vector", s("ta"), s("ta").n)
                    dst = (hstok[:, tt2 * 8192 + cht * 128: tt2 * 8192 + (cht + 1) * 128]
                           if cht < 64 else
                           btok[:, tt2 * 1024 + (cht - 64) * 128: tt2 * 1024 + (cht - 63) * 128])
                    p.run("vector", lambda e, pr=par, d=dst: e.tensor_copy(
                        d, PTB[pr][:, :]), s("ev2"))
                    ptbf[par] = [("ev2", s("ev2").n)]
                    ntp += 1
            v_tp_ev = s("ev2").n
        with ExitStack() as esB:
            psB = lambda nm, sh, dt=F32: esB.enter_context(
                nc.psum_tensor(f"{nm}_{ch}", sh, dt))
            PDQ = psB("PDQ", [128, 1536])
            PS = [psB(f"PS{i}", [128, 384]) for i in range(2)]
            PY = [psB(f"PY{i}", [128, 256]) for i in range(2)]
            PST = psB("PST", [128, 512])
            wmulti("tensor", pdq_free + pst_free)
            wmulti("tensor", clrow_free)
            p.wait("tensor", s("pw"), s("pw").n)
            p.wait("tensor", s("ev2"), v_cum)
            c0 = ch * CS + 255
            p.run("tensor", lambda e, cc=c0: e.matmul(
                PST[0:1, 0:128], cum_fm[:, cc:cc + 1], c_identf[:, :],
                start=True, stop=True), s("ta"))
            p.wait("vector", s("ta"), s("ta").n)
            p.run("vector", lambda e: e.tensor_copy(clrow[:, :],
                                                    PST[0:1, 0:128]), s("ev2"))
            v_clrow = s("ev2").n
            p.wait("sync", s("ev2"), v_clrow)
            p.dma("sync", O["clast"][ch:ch + 1, :], clrow[:, :], s("stO"))
            v_clast_dma = s("stO").n
            p.wait("tensor", s("ev2"), v_clrow)
            p.run("tensor", lambda e: e.matmul(PST[:, 0:128], c_ones1[:, :],
                                               clrow[:, :], start=True,
                                               stop=True), s("ta"))
            v_clb = s("ta").n
            p.wait("vector", s("ta"), v_clb)
            p.wait("vector", s("ev2"), v_biastok)
            wmulti("vector", sexp_free)
            for t2 in range(2):
                p.run("vector", lambda e, tt=t2, b0=(2 * ch + t2) * 128:
                      e.tensor_add(sinb[:, tt * 128:(tt + 1) * 128],
                                   PST[:, 0:128], bias_tok[:, b0:b0 + 128]),
                      s("ev2"))
            v_sinb = s("ev2").n
            p.wait("scalar", s("ev2"), v_sinb)
            p.run("scalar", lambda e: e.activation(sexp[:, :], sinb[:, :],
                                                   AF.Exp), s("ax"))
            v_sexp = s("ax").n
            if debug:
                p.wait("sync", s("ax"), v_sexp)
                p.dma("sync", O["dbg_sexp"][ch], sexp[:, :], s("stO"))
            for q in range(32):
                g16 = q // 4
                if q % 4 == 0:
                    wmulti("sync", crow_free)
                    for ib in range(2):
                        p.dma("sync", crow[ib][:, :],
                              cumrows[ch, ib,
                                      g16 * 2048:(g16 + 1) * 2048].unsqueeze(0),
                              s("rl"))
                v_crow = s("rl").n
                p.wait("tensor", s("rl"), v_crow)
                wmulti("tensor", pdq_free)
                qo = (q % 4) * 512
                p.run("tensor", lambda e, o=qo: e.matmul(
                    PDQ[:, 0:512], c_ones1[:, :], crow[0][:, o:o + 512],
                    start=True, stop=False))
                p.run("tensor", lambda e: e.matmul(
                    PDQ[:, 0:512], c_identf[:, :], c_maskd4[:, :],
                    start=False, stop=True))
                p.run("tensor", lambda e, o=qo: e.matmul(
                    PDQ[:, 512:1024], c_ones1[:, :], crow[1][:, o:o + 512],
                    start=True, stop=True))
                p.run("tensor", lambda e, o=qo: e.matmul(
                    PDQ[:, 1024:1536], c_ones1[:, :], crow[1][:, o:o + 512],
                    start=True, stop=False))
                p.run("tensor", lambda e: e.matmul(
                    PDQ[:, 1024:1536], c_identf[:, :], c_maskd4[:, :],
                    start=False, stop=True), s("dq"))
                v_dq = s("dq").n
                crow_free = [("dq", v_dq)]
                for hh in range(4):
                    h = q * 4 + hh
                    par = h % 2
                    g = h // 16
                    wmulti("tensor", ps_free[par])
                    bt = lambda jt, gg=g: xbc_ch[:, (64 + gg) * 256 + jt * 128:
                                                 (64 + gg) * 256 + (jt + 1) * 128]
                    ctl = lambda it, gg=g: xbc_ch[:, (72 + gg) * 256 + it * 128:
                                                  (72 + gg) * 256 + (it + 1) * 128]
                    for isl, (jt, it) in enumerate([(0, 0), (0, 1), (1, 1)]):
                        p.run("tensor", lambda e, pr=par, l=bt(jt), r=ctl(it),
                              s0=isl * 128: e.matmul(
                                  PS[pr][:, s0:s0 + 128], l, r,
                                  start=True, stop=True),
                              s("sc") if isl == 2 else None, 1)
                    v_sc = s("sc").n
                    wmulti("scalar", decay_free[par])
                    p.wait("scalar", s("dq"), v_dq)
                    bj0 = (2 * ch) * 128 + h
                    bj1 = (2 * ch + 1) * 128 + h
                    in01 = PDQ[:, :].rearrange("p (a f) -> p a f", f=128)[:, hh:hh + 5:4, :]
                    o01 = decay[par][:, 0:256].rearrange("p (a f) -> p a f", f=128)
                    p.run("scalar", lambda e, i_=in01, o_=o01,
                          b=bias_tok[:, bj0:bj0 + 1]:
                          e.activation(o_, i_, AF.Exp, bias=b))
                    in2 = PDQ[:, 1024 + hh * 128:1024 + (hh + 1) * 128]
                    p.run("scalar", lambda e, i_=in2, o_=decay[par][:, 256:384],
                          b=bias_tok[:, bj1:bj1 + 1]:
                          e.activation(o_, i_, AF.Exp, bias=b), s("ax"))
                    v_ax = s("ax").n
                    wmulti("vector", msc_free[par])
                    p.wait("vector", s("ax"), v_ax)
                    p.wait("vector", s("sc"), v_sc)
                    p.run("vector", lambda e, pr=par: e.tensor_mul(
                        msc[pr][:, :], PS[pr][:, :], decay[pr][:, :]), s("ms"))
                    v_ms = s("ms").n
                    ps_free[par] = [("ms", v_ms)]
                    decay_free[par] = [("ms", v_ms)]
                    p.wait("vector", s("ax"), v_sexp)
                    wmulti("vector", hss_free); wmulti("vector", dhs_free)
                    for t2 in range(2):
                        hsl = hstok[:, t2 * 8192 + h * 64: t2 * 8192 + (h + 1) * 64]
                        p.run("vector", lambda e, o=hssb[:, t2 * 64:(t2 + 1) * 64],
                              i_=hsl, sc_=sexp[:, t2 * 128 + h: t2 * 128 + h + 1]:
                              e.tensor_scalar_mul(o, i_, sc_))
                        p.run("vector", lambda e, o=dhsb[:, t2 * 64:(t2 + 1) * 64],
                              i_=hsl, sc_=Db[:, h:h + 1]:
                              e.tensor_scalar_mul(o, i_, sc_), s("ev2"))
                    v_hd = s("ev2").n
                    p.wait("tensor", s("ev2"), v_hd)
                    if h % 8 == 0:
                        wmulti("tensor", pst_free)
                        if h == 0:
                            p.wait("tensor", s("ev2"), v_sinb)
                    for t2 in range(2):
                        p.run("tensor", lambda e, tt=t2, gg=g, o=(h % 8) * 64:
                              e.matmul(PST[:, o:o + 64],
                                       btok[:, tt * 1024 + gg * 128:
                                            tt * 1024 + (gg + 1) * 128],
                                       hssb[:, tt * 64:(tt + 1) * 64],
                                       start=(tt == 0), stop=(tt == 1)),
                              s("sm") if t2 == 1 else None, 1)
                    v_sm = s("sm").n
                    hss_free = [("sm", v_sm)]
                    p.wait("tensor", s("ms"), v_ms)
                    wmulti("tensor", py_free[par])
                    hs0 = hstok[:, 0 * 8192 + h * 64: 0 * 8192 + (h + 1) * 64]
                    hs1 = hstok[:, 1 * 8192 + h * 64: 1 * 8192 + (h + 1) * 64]
                    pb = 0
                    pyp = PY[par]
                    p.run("tensor", lambda e, l=hs0, pr=par, py=pyp: e.matmul(
                        py[0:64, 0:128], l, msc[pr][:, 0:128],
                        start=True, stop=False))
                    p.run("tensor", lambda e, py=pyp: e.matmul(
                        py[0:64, 0:128], dhsb[:, 0:64], c_identb[:, :],
                        start=False, stop=True))
                    p.run("tensor", lambda e, l=hs0, pr=par, py=pyp: e.matmul(
                        py[0:64, 128:256], l, msc[pr][:, 128:256],
                        start=True, stop=False))
                    p.run("tensor", lambda e, l=hs1, pr=par, py=pyp: e.matmul(
                        py[0:64, 128:256], l, msc[pr][:, 256:384],
                        start=False, stop=False))
                    p.run("tensor", lambda e, py=pyp: e.matmul(
                        py[0:64, 128:256], dhsb[:, 64:128], c_identb[:, :],
                        start=False, stop=True), s("ym"))
                    v_ym = s("ym").n
                    msc_free[par] = [("ym", v_ym)]
                    dhs_free = [("ym", v_ym)]
                    if debug and ch == 0:
                        p.wait("sync", s("ms"), v_ms)
                        p.dma("sync", O["dbg_msc"][h], msc[par][:, :], s("stO"))
                        msc_free[par].append(("stO", s("stO").n))
                    p.wait("vector", s("ym"), v_ym)
                    wmulti("vector", ypsb_free[par])
                    p.run("vector", lambda e, o=ypsb[par][0:64, :], pr=par:
                          e.tensor_copy(o, PY[pr][0:64, :]), s("ye"))
                    py_free[par] = [("ye", s("ye").n)]
                    p.wait("sync", s("ye"), s("ye").n)
                    p.dma("sync", O["yp"][h * 64:(h + 1) * 64,
                                          ch * CS:(ch + 1) * CS],
                          ypsb[par][0:64, :], s("stO"))
                    ypsb_free[par] = [("stO", s("stO").n)]
                    if h % 8 == 7:
                        p.wait("vector", s("sm"), v_sm)
                        wmulti("vector", sstb_free)
                        q8 = h // 8
                        p.run("vector", lambda e, qq=q8: e.tensor_copy(
                            sstb[:, :].rearrange("n (pp hh) -> n hh pp", hh=128)
                            [:, qq * 8:(qq + 1) * 8, :],
                            PST[:, :].rearrange("n (hh pp) -> n hh pp", pp=64)),
                            s("se"))
                        pst_free = [("se", s("se").n)]
                pdq_free = [("ax", s("ax").n)]
            v_se_all = s("se").n
            p.wait("sync", s("se"), v_se_all)
            p.dma("sync", O["sst"][ch], sstb[:, :], s("so"))
            v_sst_dma = s("so").n
            wmulti("scalar", cdrow_free)
            p.wait("scalar", s("ev2"), v_clrow)
            p.run("scalar", lambda e: e.activation(cdrow[:, :], clrow[:, :],
                                                   AF.Exp), s("ax"))
            v_cd = s("ax").n
            p.wait("tensor", s("ax"), v_cd)
            wmulti("tensor", pst_free)
            p.run("tensor", lambda e: e.matmul(
                PST[:, :], c_ones1[:, :],
                cdrow[:, :].unsqueeze(1).broadcast_to([1, 4, 128]),
                start=True, stop=True), s("ta"))
            v_wb = s("ta").n
            cdrow_free = [("ta", v_wb)]
            clrow_free = [("stO", v_clast_dma), ("ta", v_clb), ("ax", v_cd)]
            p.wait("vector", s("ta"), v_wb)
            p.wait("vector", s("se"), v_se_all)
            for m in range(16):
                p.run("vector", lambda e, o=m * 512: e.tensor_mul(
                    pwtmp[:, :], prev[:, o:o + 512], PST[:, :]))
                p.run("vector", lambda e, o=m * 512: e.tensor_add(
                    prev[:, o:o + 512], pwtmp[:, :], sstb[:, o:o + 512]),
                    s("pw") if m == 15 else None, 1)
            v_pw = s("pw").n
            pst_free = [("pw", v_pw)]
            sstb_free = [("so", v_sst_dma), ("pw", v_pw)]
            sexp_free = [("ev2", v_hd)]
            xbcch_free = [("sc", s("sc").n), ("ta", v_wb)]
            hstok_free = [("ym", s("ym").n), ("ev2", v_hd)]
            btok_free = [("sm", s("sm").n)]
            ptb_guard = [("ax", s("ax").n), ("ms", s("ms").n),
                         ("ye", s("ye").n), ("se", s("se").n), ("pw", v_pw)]
            pdq_free = [("ax", s("ax").n)]
    p.wait("sync", s("pw"), s("pw").n)
    p.dma("sync", O["score"][:, :], prev[:, :], s("so"))
    for nm in ("stX", "stO", "so", "rl", "cld"):
        p.wait("sync", s(nm), s(nm).n)
    p.wait("gpsimd", s("wld"), s("wld").n)
    p.wait("gpsimd", s("xld"), s("xld").n)
    p.finalize()
    es.close()
    return nc


def stage_l2_relay(r1_list):
    per = []
    for k in range(8):
        sp = np.zeros((7, 128, 8192), np.float32)
        cp = np.full((7, 4, 128), -1e4, np.float32)
        for a in range(7):
            j = k - 1 - a
            if j >= 0:
                sp[a] = r1_list[j]["score"]
                cp[a] = r1_list[j]["clast"]
        per.append(dict(
            yp=r1_list[k]["yp"], sg=r1_list[k]["sg"],
            sstin=r1_list[k]["sst"], cumin=r1_list[k]["cum"],
            clastin=r1_list[k]["clast"], cfmin=r1_list[k]["cfm"],
            spred=sp, clpred=cp))
    return per


def build_l2(debug=False):
    nc = bass.Bass("TRN2", target_bir_lowering=False, debug=False)
    es = ExitStack()
    p = Prog(nc, es)
    I = {}
    for nm, sh, dt in [
        ("yp", [HS, T], BF16), ("sg", [GATE, T], BF16),
        ("sstin", [NCH, 128, 8192], F32), ("cumin", [128, T], F32),
        ("clastin", [NCH, 128], F32), ("cfmin", [8, 128, T], BF16),
        ("spred", [7, 128, 8192], F32), ("clpred", [7, 4, 128], F32),
        ("x0tok", [T, DM], F32),
        ("wop", [8, 64, 128, 512], F32), ("wbase", [8, 32, 128, 512], F32),
        ("wdown", [32, 128, 4, 128], F32), ("wqT", [32, 128, 128], F32),
        ("normw", [128, 64], F32), ("identf", [128, 128], F32),
        ("identb", [128, 128], BF16), ("onesrow", [1, 128], F32),
        ("tri7", [7, 7], F32),
    ]:
        I[nm] = nc.dram_tensor(nm, sh, dt, kind="ExternalInput").ap()
    O = {}
    outs = [("x1", [T, DM], F32), ("basew", [T, DM], F32),
            ("hmat", [512, T], BF16), ("qp", [1, 128], F32)]
    if debug:
        outs += [("dbg_init", [128, 8192], F32), ("dbg_g", [64, 128, T], BF16),
                 ("dbg_attn", [T, DM], F32), ("dbg_ghf", [128, 64 * 512], BF16)]
    for nm, sh, dt in outs:
        O[nm] = nc.dram_tensor(nm, sh, dt, kind="ExternalOutput").ap()
    cumrows2 = nc.dram_tensor("cumrows2", [NCH, 2, 128 * 128], F32, kind="Internal").ap()
    gst = nc.dram_tensor("gst", [64, 128, T], BF16, kind="Internal").ap()
    wrows = nc.dram_tensor("wrows", [7, 128], F32, kind="Internal").ap()
    x1i = nc.dram_tensor("x1i", [T, DM], F32, kind="Internal").ap()

    sb = lambda nm, sh, dt: es.enter_context(nc.sbuf_tensor(nm, sh, dt))
    S = {nm: p.sem(nm) for nm in
         ("cld wld rl mm ta ev2 ax st so pw dq ms ym ye gq").split()}
    s = lambda nm: S[nm]

    def wmulti(eng, lst):
        for (nm, v) in lst:
            p.wait(eng, s(nm), v)

    c_identf = sb("c_identf", [128, 128], F32)
    c_identb = sb("c_identb", [128, 128], BF16)
    c_ones1 = sb("c_ones1", [1, 128], F32)
    c_onesbc = sb("c_onesbc", [128, 1], BF16)
    c_normw = sb("c_normw", [128, 64], F32)
    c_eps = sb("c_eps", [128, 1], F32)
    esP = ExitStack()
    sbP = lambda nm, sh, dt: esP.enter_context(nc.sbuf_tensor(nm, sh, dt))
    c_tri7 = sbP("c_tri7", [7, 7], F32)
    cfm_sb = sbP("cfm_sb", [128, 8 * T], BF16)
    cl_sb = sbP("cl_sb", [1, 512], F32)
    pv = sbP("pv", [128, 8192], F32)
    pvb = sbP("pvb", [128, 8192], BF16)
    spbuf = sbP("spbuf", [128, 8192], F32)
    pwtmp = sbP("pwtmp", [128, 512], F32)
    wrow_sb = sbP("wrow_sb", [1, 896], F32)
    crow = [sbP(f"crow{i}", [1, 2048], F32) for i in range(2)]
    cdr = sbP("cdr", [1, 128], F32)

    for nm, dst in [("identf", c_identf), ("identb", c_identb),
                    ("onesrow", c_ones1), ("tri7", c_tri7),
                    ("normw", c_normw)]:
        p.dma("sync", dst[:, :], I[nm][:, :], s("cld"))
    p.dma("sync", cl_sb[:, :], I["clastin"].rearrange("c h -> (c h)").unsqueeze(0),
          s("cld"))
    for h2 in range(2):
        p.dma("sync", cfm_sb[:, h2 * 4096:(h2 + 1) * 4096],
              I["cfmin"][h2 * 4:(h2 + 1) * 4].transpose([1, 0, 2]), s("cld"))
    for c in range(NCH):
        p.dma("sync", cumrows2[c].rearrange("b (h i) -> h b i", i=128),
              I["cumin"][:, c * CS:(c + 1) * CS].rearrange("h (b i) -> h b i", b=2),
              s("rl"))
    v_cld = s("cld").n
    p.run("vector", lambda e: e.memset(c_onesbc[:, :], 1.0))
    p.run("vector", lambda e: e.memset(c_eps[:, :], EPS), s("ev2"))
    clp = sbP("clp", [7, 512], F32)
    dsc = sbP("dsc", [7, 128], F32)
    dsc2 = sbP("dsc2", [7, 128], F32)
    wsb = sbP("wsb", [7, 128], F32)
    p.dma("sync", clp[:, :], I["clpred"].rearrange("a c h -> a (c h)"), s("cld"))
    p.wait("vector", s("cld"), s("cld").n)
    p.run("vector", lambda e: e.tensor_add(dsc[:, :], clp[:, 0:128],
                                           clp[:, 128:256]))
    p.run("vector", lambda e: e.tensor_add(dsc2[:, :], clp[:, 256:384],
                                           clp[:, 384:512]))
    p.run("vector", lambda e: e.tensor_add(dsc[:, :], dsc[:, :], dsc2[:, :]),
          s("ev2"))
    with ExitStack() as es0:
        PW = es0.enter_context(nc.psum_tensor("PW", [7, 128], F32))
        p.wait("tensor", s("cld"), v_cld)
        p.wait("tensor", s("ev2"), s("ev2").n)
        p.run("tensor", lambda e: e.matmul(PW[:, :], c_tri7[:, :], dsc[:, :],
                                           start=True, stop=True), s("ta"))
        p.wait("scalar", s("ta"), s("ta").n)
        p.run("scalar", lambda e: e.activation(wsb[:, :], PW[:, :], AF.Exp),
              s("ax"))
        p.wait("sync", s("ax"), s("ax").n)
        p.dma("sync", wrows[:, :], wsb[:, :], s("so"))
        p.wait("sync", s("so"), s("so").n)
        p.dma("sync", wrow_sb[:, 0:896],
              wrows.rearrange("a h -> (a h)").unsqueeze(0), s("rl"))
    v_wrow = s("rl").n
    p.run("vector", lambda e: e.memset(pv[:, :], 0.0), s("pw"))
    with ExitStack() as esI:
        PB2 = esI.enter_context(nc.psum_tensor("PB2i", [128, 512], F32))
        for a in range(7):
            p.wait("sync", s("pw"), s("pw").n)
            p.dma("sync", spbuf[:, :], I["spred"][a].rearrange("p f -> p f"),
                  s("rl"))
            v_sp = s("rl").n
            p.wait("tensor", s("rl"), v_wrow)
            p.wait("tensor", s("pw"), s("pw").n)
            p.run("tensor", lambda e, aa=a: e.matmul(
                PB2[:, :], c_ones1[:, :],
                wrow_sb[:, aa * 128:(aa + 1) * 128].unsqueeze(1)
                .broadcast_to([1, 4, 128]), start=True, stop=True), s("ta"))
            p.wait("vector", s("ta"), s("ta").n)
            p.wait("vector", s("rl"), v_sp)
            for m in range(16):
                p.run("vector", lambda e, o=m * 512: e.tensor_mul(
                    pwtmp[:, :], spbuf[:, o:o + 512], PB2[:, :]))
                p.run("vector", lambda e, o=m * 512: e.tensor_add(
                    pv[:, o:o + 512], pv[:, o:o + 512], pwtmp[:, :]),
                    s("pw") if m == 15 else None, 1)
    v_init = s("pw").n
    if debug:
        p.wait("sync", s("pw"), v_init)
        p.dma("sync", O["dbg_init"][:, :], pv[:, :], s("so"))
    p.finalize_marker = None
    return _l2_tail(nc, es, p, I, O, cumrows2, gst, x1i, sb, S, debug,
                    dict(c_identf=c_identf, c_identb=c_identb,
                         c_ones1=c_ones1, c_onesbc=c_onesbc, c_normw=c_normw,
                         cfm_sb=cfm_sb, cl_sb=cl_sb, pv=pv, pvb=pvb,
                         c_eps=c_eps,
                         spbuf=spbuf, pwtmp=pwtmp, crow=crow, cdr=cdr,
                         esP=esP),
                    v_init)


def _l2_tail(nc, es, p, I, O, cumrows2, gst, x1i, sb, S, debug, R, v_init):
    s = lambda nm: S[nm]
    c_identf, c_identb = R["c_identf"], R["c_identb"]
    c_ones1, c_onesbc, c_normw = R["c_ones1"], R["c_onesbc"], R["c_normw"]
    cfm_sb, cl_sb = R["cfm_sb"], R["cl_sb"]
    pv, pvb, spbuf, pwtmp = R["pv"], R["pvb"], R["spbuf"], R["pwtmp"]
    crow, cdr = R["crow"], R["cdr"]

    def wmulti(eng, lst):
        for (nm, v) in lst:
            p.wait(eng, s(nm), v)

    esY = ExitStack()
    sbY = lambda nm, sh, dt: esY.enter_context(nc.sbuf_tensor(nm, sh, dt))
    ypch = sbY("ypch", [128, 64 * 256], BF16)
    sgch = sbY("sgch", [128, 64 * 256], BF16)
    ecb = [sbY(f"ecb{i}", [128, 256], BF16) for i in range(2)]
    ctb = [sbY(f"ctb{i}", [128, 256], BF16) for i in range(2)]
    ts1 = sbY("ts1", [128, 256], F32)
    ggrp = sbY("ggrp", [128, 8 * 256], BF16)
    sqb = sbY("sqb", [128, 256], BF16)
    rsb = sbY("rsb", [1, 256], F32)
    gfm = [sbY(f"gfm{i}", [128, 256], BF16) for i in range(2)]
    esYp = ExitStack()
    PU = esYp.enter_context(nc.psum_tensor("PU", [128, 1024], F32))
    PYI = [esYp.enter_context(nc.psum_tensor(f"PYI{i}", [128, 256], F32))
           for i in range(2)]
    PG = esYp.enter_context(nc.psum_tensor("PG", [1, 256], F32))
    PB2 = esYp.enter_context(nc.psum_tensor("PB2", [128, 512], F32))

    ecb_free = [[], []]; ctb_free = [[], []]; pyi_free = [[], []]
    pu_free = []; crow_free = []; gfm_free = [[], []]
    pg_free = []; pb2_free = []
    v_pvb = 0

    for ch in range(NCH):
        p.wait("vector", s("pw"), s("pw").n)
        wmulti("vector", [("ym", s("ym").n), ("ax", s("ax").n),
                          ("ye", s("ye").n)])
        p.run("vector", lambda e: e.tensor_copy(pvb[:, :], pv[:, :]), s("ev2"))
        v_pvb = s("ev2").n
        wmulti("sync", [("ym", s("ym").n), ("ms", s("ms").n)])
        for d8 in range(8):
            p.dma("sync", ypch[:, d8 * 8 * 256:(d8 + 1) * 8 * 256],
                  I["yp"].rearrange("(a p) t -> a p t", p=128)
                  [d8 * 8:(d8 + 1) * 8, :, ch * CS:(ch + 1) * CS]
                  .transpose([1, 0, 2]), s("rl"))
            p.dma("sync", sgch[:, d8 * 8 * 256:(d8 + 1) * 8 * 256],
                  I["sg"].rearrange("(a p) t -> a p t", p=128)
                  [d8 * 8:(d8 + 1) * 8, :, ch * CS:(ch + 1) * CS]
                  .transpose([1, 0, 2]), s("rl"))
        v_rly = s("rl").n
        p.wait("vector", s("rl"), v_rly)
        for q in range(32):
            g16 = q // 4
            if q % 4 == 0:
                wmulti("sync", crow_free)
                for ib in range(2):
                    p.dma("sync", crow[ib][:, :],
                          cumrows2[ch, ib, g16 * 2048:(g16 + 1) * 2048]
                          .unsqueeze(0), s("rl"))
            v_crow = s("rl").n
            p.wait("tensor", s("rl"), v_crow)
            wmulti("tensor", pu_free)
            qo = (q % 4) * 512
            p.run("tensor", lambda e, o=qo: e.matmul(
                PU[:, 0:512], c_ones1[:, :], crow[0][:, o:o + 512],
                start=True, stop=True))
            p.run("tensor", lambda e, o=qo: e.matmul(
                PU[:, 512:1024], c_ones1[:, :], crow[1][:, o:o + 512],
                start=True, stop=True), s("dq"))
            v_dq = s("dq").n
            crow_free = [("dq", v_dq)]
            for hh in range(4):
                h = q * 4 + hh
                par = h % 2
                g = h // 16
                cht = h // 2
                b = (h % 2) * 64
                wmulti("scalar", ecb_free[par])
                p.wait("scalar", s("dq"), v_dq)
                inu = PU[:, :].rearrange("p (a f) -> p a f", f=128)[:, hh:hh + 5:4, :]
                p.run("scalar", lambda e, i_=inu, o_=ecb[par][:, :]
                      .rearrange("p (a f) -> p a f", f=128):
                      e.activation(o_, i_, AF.Exp), s("ax"))
                v_ax = s("ax").n
                wmulti("vector", ctb_free[par])
                p.wait("vector", s("ax"), v_ax)
                cslice = cfm_sb[:, g * T + ch * CS: g * T + (ch + 1) * CS]
                p.run("vector", lambda e, pr=par, cs=cslice: e.tensor_mul(
                    ctb[pr][:, :], cs, ecb[pr][:, :]), s("ms"))
                v_ms = s("ms").n
                ecb_free[par] = [("ym", 0), ("ms", v_ms)]
                p.wait("tensor", s("ms"), v_ms)
                p.wait("tensor", s("ev2"), v_pvb)
                wmulti("tensor", pyi_free[par])
                lhs = pvb[:, :].rearrange("n (pp hh) -> n hh pp", hh=128)[:, h, :]
                if b:
                    p.run("tensor", lambda e, l=lhs, pr=par: e.matmul(
                        PYI[pr][64:128, :], l, ctb[pr][:, :],
                        start=True, stop=True, tile_position=(0, 64)), s("ym"))
                else:
                    p.run("tensor", lambda e, l=lhs, pr=par: e.matmul(
                        PYI[pr][0:64, :], l, ctb[pr][:, :],
                        start=True, stop=True), s("ym"))
                v_ym = s("ym").n
                ctb_free[par] = [("ym", v_ym)]
                p.wait("vector", s("ym"), v_ym)
                ypsl = ypch[b:b + 64, cht * 256:(cht + 1) * 256]
                sgsl = sgch[b:b + 64, cht * 256:(cht + 1) * 256]
                ggsl = ggrp[b:b + 64, (cht % 8) * 256:(cht % 8 + 1) * 256]
                p.run("vector", lambda e, pr=par, yy=ypsl, bb=b: e.tensor_add(
                    ts1[bb:bb + 64, :], PYI[pr][bb:bb + 64, :], yy))
                p.run("vector", lambda e, gg=ggsl, ss=sgsl, bb=b: e.tensor_mul(
                    gg, ts1[bb:bb + 64, :], ss), s("ms"))
                pyi_free[par] = [("ms", s("ms").n)]
                if h % 2 == 1:
                    p.wait("scalar", s("ms"), s("ms").n)
                    p.run("scalar", lambda e, gg=ggrp[:, (cht % 8) * 256:
                                                      (cht % 8 + 1) * 256]:
                          e.activation(sqb[:, :], gg, AF.Square), s("ax"))
                    p.wait("tensor", s("ax"), s("ax").n)
                    if cht % 8 == 0:
                        wmulti("tensor", pg_free)
                    p.run("tensor", lambda e, st=(cht % 8 == 0): e.matmul(
                        PG[:, :], c_onesbc[:, :], sqb[:, :],
                        start=st, stop=(cht % 8 == 7)), s("gq"))
                    if cht % 8 == 7:
                        v_gq = s("gq").n
                        p.wait("scalar", s("gq"), v_gq)
                        p.run("scalar", lambda e: e.activation(
                            rsb[:, :], PG[:, :], AF.Ln, scale=1.0 / 1024,
                            bias=R["c_eps"][0:1, :]))
                        p.run("scalar", lambda e: e.activation(
                            rsb[:, :], rsb[:, :], AF.Exp, scale=-0.5), s("ax"))
                        v_rstd = s("ax").n
                        pg_free = [("ax", v_rstd)]
                        p.wait("tensor", s("ax"), v_rstd)
                        wmulti("tensor", pb2_free)
                        p.run("tensor", lambda e: e.matmul(
                            PB2[:, 0:256], c_ones1[:, :], rsb[:, :],
                            start=True, stop=True), s("ta"))
                        v_pb2 = s("ta").n
                        p.wait("vector", s("ta"), v_pb2)
                        for t8 in range(8):
                            chtt = (cht // 8) * 8 + t8
                            parg = t8 % 2
                            wmulti("vector", gfm_free[parg])
                            p.run("vector", lambda e, gg=ggrp[:, t8 * 256:
                                                              (t8 + 1) * 256],
                                  w=c_normw[:, chtt:chtt + 1], pr=parg:
                                  e.scalar_tensor_tensor(
                                      gfm[pr][:, :], gg, w, PB2[:, 0:256],
                                      ALU.mult, ALU.mult), s("ye"))
                            v_gf = s("ye").n
                            p.wait("sync", s("ye"), v_gf)
                            p.dma("sync", gst[chtt][:, ch * CS:(ch + 1) * CS],
                                  gfm[parg][:, :], s("st"))
                            gfm_free[parg] = [("st", s("st").n)]
                            if debug:
                                p.dma("sync", O["dbg_g"][chtt]
                                      [:, ch * CS:(ch + 1) * CS],
                                      gfm[parg][:, :], s("st"))
                                gfm_free[parg].append(("st", s("st").n))
                        pb2_free = [("ye", s("ye").n)]
            pu_free = [("ax", s("ax").n)]
        if ch < NCH - 1:
            p.wait("sync", s("pw"), s("pw").n)
            p.dma("sync", spbuf[:, :], I["sstin"][ch], s("rl"))
            v_sst = s("rl").n
            p.wait("scalar", s("cld"), s("cld").n)
            p.run("scalar", lambda e, c0=ch * 128: e.activation(
                cdr[:, :], cl_sb[0:1, c0:c0 + 128], AF.Exp), s("ax"))
            v_cd = s("ax").n
            p.wait("tensor", s("ax"), v_cd)
            wmulti("tensor", pb2_free)
            p.run("tensor", lambda e: e.matmul(
                PB2[:, :], c_ones1[:, :],
                cdr[:, :].unsqueeze(1).broadcast_to([1, 4, 128]),
                start=True, stop=True), s("ta"))
            v_wb = s("ta").n
            p.wait("vector", s("ta"), v_wb)
            p.wait("vector", s("rl"), v_sst)
            p.wait("vector", s("ym"), s("ym").n)
            for m in range(16):
                p.run("vector", lambda e, o=m * 512: e.tensor_mul(
                    pwtmp[:, :], pv[:, o:o + 512], PB2[:, :]))
                p.run("vector", lambda e, o=m * 512: e.tensor_add(
                    pv[:, o:o + 512], pwtmp[:, :], spbuf[:, o:o + 512]),
                    s("pw") if m == 15 else None, 1)
            pb2_free = [("pw", s("pw").n)]
    v_y_done = dict(st=s("st").n, ye=s("ye").n, ms=s("ms").n, ym=s("ym").n,
                    ax=s("ax").n, rl=s("rl").n)
    esYp.close()
    esY.close()
    R["esP"].close()
    return _l2_outproj(nc, es, p, I, O, gst, x1i, sb, S, debug, R, v_y_done)


def _l2_outproj(nc, es, p, I, O, gst, x1i, sb, S, debug, R, v_y):
    s = lambda nm: S[nm]
    c_identf, c_ones1, c_onesbc = R["c_identf"], R["c_ones1"], R["c_onesbc"]

    def wmulti(eng, lst):
        for (nm, v) in lst:
            p.wait(eng, s(nm), v)

    es2 = ExitStack()
    sb2 = lambda nm, sh, dt: es2.enter_context(nc.sbuf_tensor(nm, sh, dt))
    ghf = sb2("ghf", [128, 64 * 512], BF16)
    wopb = [sb2(f"wopb{i}", [128, 16 * 512], BF16) for i in range(2)]
    x2a = sb2("x2a", [128, 4 * 4096], F32)
    x0t = sb2("x0t", [128, 4096], F32)
    lnt = sb2("lnt", [128, 4096], F32)
    mcol = sb2("mcol", [128, 1], F32)
    mcol2 = sb2("mcol2", [128, 1], F32)
    vcol = sb2("vcol", [128, 1], F32)
    es2p = ExitStack()
    PO = [es2p.enter_context(nc.psum_tensor(f"PO{i}", [128, 512], F32))
          for i in range(8)]
    for eng in ("vector", "scalar", "sync", "tensor", "gpsimd"):
        for nm in S:
            p.wait(eng, s(nm), s(nm).n)
    wop_free = [[], []]
    po_free = [[] for _ in range(8)]
    ghf_free = []
    x2a_free = []
    for th in range(2):
        wmulti("sync", ghf_free)
        for c8 in range(8):
            p.dma("sync", ghf[:, c8 * 8 * 512:(c8 + 1) * 8 * 512],
                  gst[c8 * 8:(c8 + 1) * 8, :, th * 512:(th + 1) * 512]
                  .transpose([1, 0, 2]), s("rl"))
        v_g = s("rl").n
        if debug and th == 0:
            p.wait("sync", s("rl"), v_g)
            p.dma("sync", O["dbg_ghf"][:, :], ghf[:, :], s("so"))
        for dc in range(8):
            pp = dc % 2
            for kg in range(4):
                wmulti("gpsimd", wop_free[kg % 2])
                p.dma("gpsimd",
                      wopb[kg % 2][:, :].rearrange("p (a j) -> p a j", a=16),
                      I["wop"][dc, kg * 16:(kg + 1) * 16]
                      .transpose([1, 0, 2]), s("wld"))
                v_w = s("wld").n
                p.wait("tensor", s("wld"), v_w)
                if kg == 0:
                    p.wait("tensor", s("rl"), v_g)
                    for t4 in range(4):
                        wmulti("tensor", po_free[pp * 4 + t4])
                for t4 in range(4):
                    for k in range(16):
                        kk = kg * 16 + k
                        lhs = ghf[:, kk * 512 + t4 * 128: kk * 512 + (t4 + 1) * 128]
                        p.run("tensor", lambda e, l=lhs, r=wopb[kg % 2]
                              [:, k * 512:(k + 1) * 512],
                              po=PO[pp * 4 + t4][:, :],
                              st=(kg == 0 and k == 0), sp=(kg == 3 and k == 15):
                              e.matmul(po, l, r, start=st, stop=sp),
                              s("mm") if (k == 15 and t4 == 3)
                              else None, 1)
                wop_free[kg % 2] = [("mm", s("mm").n)]
            v_mm = s("mm").n
            p.wait("vector", s("mm"), v_mm)
            wmulti("vector", x2a_free)
            for t4 in range(4):
                p.run("vector", lambda e, po=PO[pp * 4 + t4][:, :],
                      o=x2a[:, t4 * 4096 + dc * 512: t4 * 4096 + (dc + 1) * 512]:
                      e.tensor_copy(o, po), s("ev2"))
                po_free[pp * 4 + t4] = [("ev2", s("ev2").n)]
                if debug:
                    p.wait("sync", s("ev2"), s("ev2").n)
                    p.dma("sync", O["dbg_attn"]
                          [th * 512 + t4 * 128: th * 512 + (t4 + 1) * 128,
                           dc * 512:(dc + 1) * 512],
                          x2a[:, t4 * 4096 + dc * 512: t4 * 4096 + (dc + 1) * 512],
                          s("so"))
        v_evall = s("ev2").n
        x2a_free = []
        for t4 in range(4):
            row0 = th * 512 + t4 * 128
            p.wait("sync", s("gq"), s("gq").n)
            p.dma("sync", x0t[:, :], I["x0tok"][row0:row0 + 128, :], s("rl"))
            v_x0 = s("rl").n
            p.wait("vector", s("rl"), v_x0)
            p.wait("vector", s("ev2"), v_evall)
            p.run("vector", lambda e, a=x2a[:, t4 * 4096:(t4 + 1) * 4096]:
                  e.tensor_add(lnt[:, :], a, x0t[:, :]))
            p.run("vector", lambda e: e.reduce_sum(mcol[:, :], lnt[:, :],
                                                   axis=AX.X))
            p.run("vector", lambda e: e.memset(vcol[:, :], 0.0), s("ev2"))
            p.wait("scalar", s("ev2"), s("ev2").n)
            p.run("scalar", lambda e: e.activation(mcol2[:, :], mcol[:, :],
                                                   AF.Identity, scale=1.0 / DM),
                  s("ax"))
            p.wait("vector", s("ax"), s("ax").n)
            p.run("vector", lambda e, a=x2a[:, t4 * 4096:(t4 + 1) * 4096]:
                  e.tensor_scalar(a, lnt[:, :], mcol2[:, :], None,
                                  ALU.subtract), s("ev2"))
            x2a_free.append(("ev2", s("ev2").n))
            p.wait("scalar", s("ev2"), s("ev2").n)
            p.run("scalar", lambda e, a=x2a[:, t4 * 4096:(t4 + 1) * 4096]:
                  e.activation(ghf[:, 0:8192].bitcast(F32), a, AF.Square,
                               accum_out=vcol[:, :]))
            p.run("scalar", lambda e: e.activation(vcol[:, :], vcol[:, :],
                                                   AF.Ln, scale=1.0 / DM,
                                                   bias=R["c_eps"][:, :]))
            p.run("scalar", lambda e: e.activation(vcol[:, :], vcol[:, :],
                                                   AF.Exp, scale=-0.5), s("ax"))
            p.wait("vector", s("ax"), s("ax").n)
            p.run("vector", lambda e, a=x2a[:, t4 * 4096:(t4 + 1) * 4096]:
                  e.tensor_scalar_mul(lnt[:, :], a, vcol[:, :]), s("ev2"))
            v_x1 = s("ev2").n
            p.wait("sync", s("ev2"), v_x1)
            p.dma("sync", O["x1"][row0:row0 + 128, :], lnt[:, :], s("so"))
            p.dma("sync", x1i[row0:row0 + 128, :], lnt[:, :], s("st"))
            x2a_free.append(("so", s("so").n))
            x2a_free.append(("ax", s("ax").n))
            p.run("sync", lambda e: e.nop(), s("gq"), 1)
        ghf_free = [("mm", s("mm").n), ("ax", s("ax").n)]
    v_x1_all = s("st").n
    es2p.close()
    es2.close()
    return _l2_ffn(nc, es, p, I, O, x1i, sb, S, debug, R, v_x1_all)


def _l2_ffn(nc, es, p, I, O, x1i, sb, S, debug, R, v_x1_all):
    s = lambda nm: S[nm]
    c_identf, c_onesbc = R["c_identf"], R["c_onesbc"]

    def wmulti(eng, lst):
        for (nm, v) in lst:
            p.wait(eng, s(nm), v)

    es3 = ExitStack()
    sb3 = lambda nm, sh, dt: es3.enter_context(nc.sbuf_tensor(nm, sh, dt))
    x1fm = sb3("x1fm", [128, 32 * 1024], BF16)
    x1t = sb3("x1t", [128, 4096], F32)
    wb2 = sb3("wb2", [128, 32 * 512], BF16)
    wq_sb = sb3("wq_sb", [128, 32 * 128], BF16)
    wd_sb = sb3("wd_sb", [128, 32 * 512], BF16)
    bout = [sb3(f"bout{i}", [128, 512], F32) for i in range(2)]
    qxt = sb3("qxt", [128, 128], BF16)
    qrow = sb3("qrow", [1, 128], F32)
    hout = [sb3(f"hout{i}", [128, 512], BF16) for i in range(2)]
    es3p = ExitStack()
    PTF = [es3p.enter_context(nc.psum_tensor(f"PTF{i}", [128, 128], F32))
           for i in range(2)]
    POB = [es3p.enter_context(nc.psum_tensor(f"POB{i}", [128, 512], F32))
           for i in range(2)]
    PQX = es3p.enter_context(nc.psum_tensor("PQX", [128, 128], F32))
    PQ = es3p.enter_context(nc.psum_tensor("PQ", [1, 128], F32))
    PH2_ = [es3p.enter_context(nc.psum_tensor(f"PH{i}", [128, 512], F32))
            for i in range(2)]
    for eng in ("vector", "scalar", "sync", "tensor", "gpsimd"):
        for nm in S:
            p.wait(eng, s(nm), s(nm).n)
    p.dma("gpsimd", wq_sb[:, :].rearrange("p (a r) -> p a r", a=32),
          I["wqT"].transpose([1, 0, 2]), s("wld"))
    p.dma("gpsimd", wd_sb[:, :].rearrange("p (a b r) -> p a b r", a=32, b=4),
          I["wdown"].transpose([1, 0, 2, 3]), s("wld"))
    v_wq = s("wld").n
    ptf_free = [[], []]
    for t8 in range(8):
        p.wait("sync", s("ta"), s("ta").n)
        p.dma("sync", x1t[:, :], x1i[t8 * 128:(t8 + 1) * 128, :], s("rl"))
        v_x = s("rl").n
        p.wait("tensor", s("rl"), v_x)
        for dt in range(32):
            pr = dt % 2
            wmulti("tensor", ptf_free[pr])
            p.run("tensor", lambda e, d=dt, pp=pr: e.transpose(
                PTF[pp][:, :], x1t[:, d * 128:(d + 1) * 128], c_identf[:, :]),
                s("ta"))
            p.wait("vector", s("ta"), s("ta").n)
            p.run("vector", lambda e, pp=pr, o=x1fm
                  [:, dt * 1024 + t8 * 128: dt * 1024 + (t8 + 1) * 128]:
                  e.tensor_copy(o, PTF[pp][:, :]), s("ev2"))
            ptf_free[pr] = [("ev2", s("ev2").n)]
    v_x1fm = s("ev2").n
    wb_free = []
    pob_free = [[], []]
    bout_free = [[], []]
    for oc in range(8):
        wmulti("gpsimd", wb_free)
        p.dma("gpsimd", wb2[:, :].rearrange("p (a j) -> p a j", a=32),
              I["wbase"][oc].transpose([1, 0, 2]), s("wld"))
        v_w = s("wld").n
        p.wait("tensor", s("wld"), v_w)
        p.wait("tensor", s("ev2"), v_x1fm)
        for t8 in range(8):
            pr = t8 % 2
            wmulti("tensor", pob_free[pr])
            for dt in range(32):
                p.run("tensor", lambda e, d=dt, t=t8, po=POB[pr][:, :]: e.matmul(
                    po, x1fm[:, d * 1024 + t * 128: d * 1024 + (t + 1) * 128],
                    wb2[:, d * 512:(d + 1) * 512],
                    start=(d == 0), stop=(d == 31)),
                    s("mm") if dt == 31 else None, 1)
            v_mm = s("mm").n
            p.wait("vector", s("mm"), v_mm)
            wmulti("vector", bout_free[pr])
            p.run("vector", lambda e, po=POB[pr][:, :], o=bout[pr][:, :]:
                  e.tensor_copy(o, po), s("ev2"))
            pob_free[pr] = [("ev2", s("ev2").n)]
            p.wait("sync", s("ev2"), s("ev2").n)
            p.dma("sync", O["basew"][t8 * 128:(t8 + 1) * 128,
                                     oc * 512:(oc + 1) * 512],
                  bout[pr][:, :], s("so"))
            bout_free[pr] = [("so", s("so").n)]
        wb_free = [("mm", s("mm").n)]
    p.wait("tensor", s("wld"), v_wq)
    for t8 in range(8):
        p.wait("tensor", s("ev2"), s("ev2").n if t8 == 0 else 0)
        for dt in range(32):
            p.run("tensor", lambda e, d=dt, t=t8: e.matmul(
                PQX[:, :], x1fm[:, d * 1024 + t * 128: d * 1024 + (t + 1) * 128],
                wq_sb[:, d * 128:(d + 1) * 128],
                start=(d == 0), stop=(d == 31)), s("mm") if dt == 31 else None, 1)
        v_mm = s("mm").n
        p.wait("vector", s("mm"), v_mm)
        p.wait("vector", s("gq"), s("gq").n)
        p.run("vector", lambda e: e.tensor_copy(qxt[:, :], PQX[:, :]), s("ev2"))
        v_q = s("ev2").n
        p.wait("tensor", s("ev2"), v_q)
        p.run("tensor", lambda e, st=(t8 == 0): e.matmul(
            PQ[:, :], c_onesbc[:, :], qxt[:, :], start=st, stop=(t8 == 7)),
            s("gq"))
    p.wait("vector", s("gq"), s("gq").n)
    p.run("vector", lambda e: e.tensor_copy(qrow[:, :], PQ[:, :]), s("ev2"))
    p.wait("sync", s("ev2"), s("ev2").n)
    p.dma("sync", O["qp"][:, :], qrow[:, :], s("so"))
    ph_free = [[], []]
    hout_free = [[], []]
    for rt in range(4):
        for th2 in range(2):
            pr = th2
            wmulti("tensor", ph_free[pr])
            for dt in range(32):
                p.run("tensor", lambda e, d=dt, r=rt, t=th2, po=PH2_[pr][:, :]:
                      e.matmul(po,
                               wd_sb[:, d * 512 + r * 128: d * 512 + (r + 1) * 128],
                               x1fm[:, d * 1024 + t * 512: d * 1024 + (t + 1) * 512],
                               start=(d == 0), stop=(d == 31)),
                      s("mm") if dt == 31 else None, 1)
            v_mm = s("mm").n
            p.wait("vector", s("mm"), v_mm)
            wmulti("vector", hout_free[pr])
            p.run("vector", lambda e, po=PH2_[pr][:, :], o=hout[pr][:, :]:
                  e.tensor_copy(o, po), s("ev2"))
            ph_free[pr] = [("ev2", s("ev2").n)]
            p.wait("sync", s("ev2"), s("ev2").n)
            p.dma("sync", O["hmat"][rt * 128:(rt + 1) * 128,
                                    th2 * 512:(th2 + 1) * 512],
                  hout[pr][:, :], s("so"))
            hout_free[pr] = [("so", s("so").n)]
    for nm in ("so", "st", "rl", "cld", "wld"):
        p.wait("sync", s(nm), s(nm).n)
    p.finalize()
    es3p.close()
    es3.close()
    es.close()
    return nc


def build_l3(debug=False):
    nc = bass.Bass("TRN2", target_bir_lowering=False, debug=False)
    es = ExitStack()
    p = Prog(nc, es)
    I = {}
    for nm, sh, dt in [
        ("qpall", [1, 1024], F32), ("keysrow", [1, 1024], F32),
        ("x1", [T, DM], F32), ("basew", [T, DM], F32),
        ("hmat", [512, T], BF16), ("wup", [4, 128, 8, 512], F32),
        ("onesrow", [1, 128], F32),
    ]:
        I[nm] = nc.dram_tensor(nm, sh, dt, kind="ExternalInput").ap()
    O = {}
    outs = [("out", [T, DM], F32)]
    if debug:
        outs += [("dbg_coef", [1, 8], F32), ("dbg_q", [1, 128], F32),
                 ("dbg_sc", [1, 8], F32), ("dbg_ex", [1, 8], F32)]
    for nm, sh, dt in outs:
        O[nm] = nc.dram_tensor(nm, sh, dt, kind="ExternalOutput").ap()
    sb = lambda nm, sh, dt: es.enter_context(nc.sbuf_tensor(nm, sh, dt))
    S = {nm: p.sem(nm) for nm in "cld wld rl mm ta ev2 ax so gq".split()}
    s = lambda nm: S[nm]

    def wmulti(eng, lst):
        for (nm, v) in lst:
            p.wait(eng, s(nm), v)

    c_ones1 = sb("c_ones1", [1, 128], F32)
    c_one11 = sb("c_one11", [1, 1], F32)
    c_eps = sb("c_eps", [128, 1], F32)
    keysb = sb("keysb", [1, 1024], F32)
    kq = sb("kq", [1, 1024], F32)
    qsb = sb("qsb", [1, 1024], F32)
    qrow = sb("qrow", [1, 128], F32)
    qcolb = sb("qcolb", [128, 1], F32)
    scrow = sb("scrow", [1, 8], F32)
    nmax = sb("nmax", [1, 1], F32)
    scdbg = sb("scdbg", [1, 8], F32)
    scdbg2 = sb("scdbg2", [1, 8], F32)
    ssum = sb("ssum", [1, 1], F32)
    srec = sb("srec", [1, 1], F32)
    ccols = sb("ccols", [128, 8], F32)
    hsb = sb("hsb", [128, 4 * T], BF16)
    hsc = sb("hsc", [128, 4 * T], BF16)
    wupb = sb("wupb", [128, 4 * 4096], BF16)
    x1sb = sb("x1sb", [128, DM], F32)
    bsb = sb("bsb", [128, DM], F32)
    x3 = sb("x3", [128, DM], F32)
    lnt = sb("lnt", [128, DM], F32)
    sqd = sb("sqd", [128, DM], F32)
    mcol = sb("mcol", [128, 1], F32)
    mcol2 = sb("mcol2", [128, 1], F32)
    vcol = sb("vcol", [128, 1], F32)
    otb = [sb(f"otb{i}", [128, DM], F32) for i in range(2)]

    p.dma("sync", c_ones1[:, :], I["onesrow"][:, :], s("cld"))
    p.dma("sync", qsb[:, :], I["qpall"][:, :], s("cld"))
    p.dma("sync", keysb[:, :], I["keysrow"][:, :], s("cld"))
    p.dma("gpsimd", hsb[:, :].rearrange("p (a t) -> p a t", a=4),
          I["hmat"].rearrange("(a p) t -> a p t", p=128).transpose([1, 0, 2]),
          s("wld"))
    p.dma("gpsimd", wupb[:, :].rearrange("p (a b j) -> p a b j", a=4, b=8),
          I["wup"].transpose([1, 0, 2, 3]), s("wld"))
    v_ld = s("cld").n
    v_wld = s("wld").n
    p.run("vector", lambda e: e.memset(c_one11[:, :], 1.0))
    p.run("vector", lambda e: e.memset(c_eps[:, :], EPS))
    p.wait("vector", s("cld"), v_ld)
    p.run("vector", lambda e: e.tensor_add(qrow[:, :], qsb[0:1, 0:128],
                                           qsb[0:1, 128:256]))
    for a in range(2, 8):
        p.run("vector", lambda e, o=a * 128: e.tensor_add(
            qrow[:, :], qrow[:, :], qsb[0:1, o:o + 128]))
    p.run("vector", lambda e: e.nop(), s("ev2"))
    with ExitStack() as esQ:
        PCB = esQ.enter_context(nc.psum_tensor("PCB", [128, 8], F32))
        p.wait("vector", s("ev2"), s("ev2").n)
        for e_ in range(8):
            p.run("vector", lambda e, o=kq[0:1, e_ * 128:(e_ + 1) * 128],
                  k_=keysb[0:1, e_ * 128:(e_ + 1) * 128]:
                  e.tensor_mul(o, k_, qrow[:, :]))
            p.run("vector", lambda e, o=scrow[0:1, e_:e_ + 1],
                  i_=kq[0:1, e_ * 128:(e_ + 1) * 128]:
                  e.reduce_sum(o, i_, axis=AX.X),
                  s("ev2") if e_ == 7 else None, 1)
        p.wait("scalar", s("ev2"), s("ev2").n)
        p.run("scalar", lambda e: e.activation(scrow[:, :], scrow[:, :],
                                               AF.Identity,
                                               scale=float(1.0 / (8192.0 * np.sqrt(128)))),
              s("ax"))
        if debug:
            p.run("scalar", lambda e: e.copy(scdbg[:, :], scrow[:, :]), s("ax"))
            p.wait("sync", s("ax"), s("ax").n)
            p.dma("sync", O["dbg_q"][:, :], qrow[:, :], s("so"))
            p.dma("sync", O["dbg_sc"][:, :], scdbg[:, :], s("so"))
        p.wait("vector", s("ax"), s("ax").n)
        p.run("vector", lambda e: e.reduce_max(nmax[:, :], scrow[:, :],
                                               axis=AX.X, negate=True),
              s("ev2"))
        p.wait("scalar", s("ev2"), s("ev2").n)
        p.run("scalar", lambda e: e.activation(scrow[:, :], scrow[:, :], AF.Exp,
                                               bias=nmax[:, :]), s("ax"))
        if debug:
            p.run("scalar", lambda e: e.copy(scdbg2[:, :], scrow[:, :]), s("ax"))
        p.wait("vector", s("ax"), s("ax").n)
        p.run("vector", lambda e: e.reduce_sum(ssum[:, :], scrow[:, :],
                                               axis=AX.X), s("ev2"))
        p.wait("scalar", s("ev2"), s("ev2").n)
        p.run("scalar", lambda e: e.activation(srec[:, :], ssum[:, :], AF.Ln))
        p.run("scalar", lambda e: e.activation(srec[:, :], srec[:, :], AF.Exp,
                                               scale=-1.0))
        p.run("scalar", lambda e: e.activation(scrow[:, :], scrow[:, :],
                                               AF.Identity,
                                               scale=srec[0:1, :]), s("ax"))
        v_cf = s("ax").n
        if debug:
            p.wait("sync", s("ax"), v_cf)
            p.dma("sync", O["dbg_coef"][:, :], scrow[:, :], s("so"))
            p.dma("sync", O["dbg_ex"][:, :], scdbg2[:, :], s("so"))
        p.wait("tensor", s("ax"), v_cf)
        p.run("tensor", lambda e: e.matmul(PCB[:, :], c_ones1[:, :],
                                           scrow[:, :], start=True, stop=True),
              s("ta"))
        p.wait("vector", s("ta"), s("ta").n)
        p.run("vector", lambda e: e.tensor_copy(ccols[:, :], PCB[:, :]),
              s("ev2"))
    v_cc = s("ev2").n
    p.wait("vector", s("wld"), v_wld)
    for rt in range(4):
        for bh in range(2):
            e_ = 2 * rt + bh
            b = bh * 64
            p.run("vector", lambda e, o=hsc[b:b + 64, rt * T:(rt + 1) * T],
                  i_=hsb[b:b + 64, rt * T:(rt + 1) * T],
                  sc_=ccols[b:b + 64, e_:e_ + 1]:
                  e.tensor_scalar_mul(o, i_, sc_), s("ev2"))
    v_hs = s("ev2").n
    with ExitStack() as esU:
        PUP = [esU.enter_context(nc.psum_tensor(f"PUP{i}", [128, 512], F32))
               for i in range(2)]
        pup_free = [[], []]
        otb_free = [[], []]
        for t8 in range(8):
            p.wait("sync", s("ax"), s("ax").n)
            p.dma("sync", x1sb[:, :], I["x1"][t8 * 128:(t8 + 1) * 128, :],
                  s("rl"))
            p.dma("sync", bsb[:, :], I["basew"][t8 * 128:(t8 + 1) * 128, :],
                  s("rl"))
            v_xb = s("rl").n
            p.wait("tensor", s("ev2"), v_hs)
            for oc in range(8):
                pr = oc % 2
                wmulti("tensor", pup_free[pr])
                for rt in range(4):
                    p.run("tensor", lambda e, r=rt, o=oc, po=PUP[pr][:, :],
                          l=hsc[:, rt * T + t8 * 128: rt * T + (t8 + 1) * 128]:
                          e.matmul(po, l,
                                   wupb[:, r * 4096 + o * 512:
                                        r * 4096 + (o + 1) * 512],
                                   start=(r == 0), stop=(r == 3)),
                          s("mm") if rt == 3 else None, 1)
                v_mm = s("mm").n
                p.wait("vector", s("mm"), v_mm)
                p.wait("vector", s("rl"), v_xb)
                p.run("vector", lambda e, po=PUP[pr][:, :],
                      o=lnt[:, oc * 512:(oc + 1) * 512],
                      b_=bsb[:, oc * 512:(oc + 1) * 512]:
                      e.tensor_add(o, po, b_))
                p.run("vector", lambda e,
                      o=x3[:, oc * 512:(oc + 1) * 512],
                      a=lnt[:, oc * 512:(oc + 1) * 512],
                      b_=x1sb[:, oc * 512:(oc + 1) * 512]:
                      e.tensor_add(o, a, b_), s("ev2"))
                pup_free[pr] = [("ev2", s("ev2").n)]
            p.run("vector", lambda e: e.tensor_copy(lnt[:, :], x3[:, :]))
            p.run("vector", lambda e: e.reduce_sum(mcol[:, :], lnt[:, :],
                                                   axis=AX.X))
            p.run("vector", lambda e: e.memset(vcol[:, :], 0.0), s("ev2"))
            p.wait("scalar", s("ev2"), s("ev2").n)
            p.run("scalar", lambda e: e.activation(mcol2[:, :], mcol[:, :],
                                                   AF.Identity, scale=1.0 / DM),
                  s("ax"))
            p.wait("vector", s("ax"), s("ax").n)
            par = t8 % 2
            wmulti("vector", otb_free[par])
            p.run("vector", lambda e, o=otb[par][:, :]: e.tensor_scalar(
                o, lnt[:, :], mcol2[:, :], None, ALU.subtract), s("ev2"))
            p.wait("scalar", s("ev2"), s("ev2").n)
            p.run("scalar", lambda e, a=otb[par][:, :]: e.activation(
                sqd[:, :], a, AF.Square, accum_out=vcol[:, :]))
            p.run("scalar", lambda e: e.activation(vcol[:, :], vcol[:, :],
                                                   AF.Ln, scale=1.0 / DM,
                                                   bias=c_eps[:, :]))
            p.run("scalar", lambda e: e.activation(vcol[:, :], vcol[:, :],
                                                   AF.Exp, scale=-0.5), s("ax"))
            p.wait("vector", s("ax"), s("ax").n)
            p.run("vector", lambda e, a=otb[par][:, :]: e.tensor_scalar_mul(
                a, a, vcol[:, :]), s("ev2"))
            v_o = s("ev2").n
            p.wait("sync", s("ev2"), v_o)
            p.dma("sync", O["out"][t8 * 128:(t8 + 1) * 128, :], otb[par][:, :],
                  s("so"))
            otb_free[par] = [("so", s("so").n)]
            p.run("sync", lambda e: e.nop(), s("ax"), 1)
    for nm in ("so", "rl", "cld", "wld"):
        p.wait("sync", s(nm), s(nm).n)
    p.finalize()
    es.close()
    return nc


def kernel(**inputs):
    x = inputs["L0_attn_input_states"][0].astype(np.float32, copy=False)
    shared1, per1 = stage_inputs(inputs)
    nc1 = build_l1(debug=False)
    im1 = []
    for k in range(8):
        d = dict(shared1); d.update(per1[k])
        im1.append(d)
    r1 = run_bass_kernel_spmd(nc1, im1, core_ids=list(range(8))).results

    relay = stage_l2_relay(r1)
    w2 = stage_l2(inputs)
    ident = np.eye(128, dtype=np.float32)
    tri7 = np.where(np.arange(7)[:, None] < np.arange(7)[None, :], 1, 0).astype(np.float32)
    shared2 = dict(w2)
    shared2.update(identf=ident, identb=ident.astype(ml_dtypes.bfloat16),
                   onesrow=np.ones((1, 128), np.float32), tri7=tri7)
    nc2 = build_l2(debug=False)
    im2 = []
    for k in range(8):
        d = dict(shared2); d.update(relay[k])
        d["x0tok"] = np.ascontiguousarray(x[1024 * k:1024 * (k + 1)])
        im2.append(d)
    r2 = run_bass_kernel_spmd(nc2, im2, core_ids=list(range(8))).results

    w3 = stage_l3(inputs)
    qpall = np.ascontiguousarray(
        np.concatenate([r2[k]["qp"].reshape(128) for k in range(8)])
        .reshape(1, 1024).astype(np.float32))
    nc3 = build_l3(debug=False)
    im3 = []
    for k in range(8):
        im3.append(dict(qpall=qpall, keysrow=w3["keysrow"], wup=w3["wup"],
                        x1=r2[k]["x1"], basew=r2[k]["basew"],
                        hmat=r2[k]["hmat"],
                        onesrow=np.ones((1, 128), np.float32)))
    r3 = run_bass_kernel_spmd(nc3, im3, core_ids=list(range(8))).results
    out = np.concatenate([r3[k]["out"] for k in range(8)], axis=0)
    return out.reshape(1, 8192, 4096).astype(np.float32, copy=False)
